# version 03
# baseline (speedup 1.0000x reference)
import contextlib

import numpy as np
import concourse.bass as bass
import concourse.mybir as mybir
from concourse.bass_utils import run_bass_kernel_spmd

F32 = mybir.dt.float32
BF16 = mybir.dt.bfloat16
I32 = mybir.dt.int32
AF = mybir.ActivationFunctionType
ALU = mybir.AluOpType

_DBG = False

D = 2048
T = 2048
R = 2048
DH = 512
DI = 1024
NE = 32
NK = 4
CAP = 512
NCT = CAP // 128
NDC = D // 128
NRC = R // 128
NIC = DI // 128
NDHC = DH // 128
TB = 512
NTB = T // TB
GU = 128 * 2048
DD = 128 * 1024
EXP_F = 16 * GU + 16 * DD
NSLOT = T * NK
PAD = 8192.0

ENGINES = ("sync", "tensor", "vector", "scalar", "gpsimd")


def _build_program():
    nc = bass.Bass("TRN2", target_bir_lowering=False)

    xt = nc.dram_tensor("xt", [D, T], F32, kind="ExternalInput").ap()
    wud = nc.dram_tensor("wud", [16, 128, 2048], F32, kind="ExternalInput").ap()
    wvd = nc.dram_tensor("wvd", [16, 128, 2048], F32, kind="ExternalInput").ap()
    wg2d = nc.dram_tensor("wg2d", [16, 128, 512], F32, kind="ExternalInput").ap()
    wg1d = nc.dram_tensor("wg1d", [128, 8192], F32, kind="ExternalInput").ap()
    wmoe = nc.dram_tensor("wmoe", [NE, EXP_F], F32, kind="ExternalInput").ap()
    bup = nc.dram_tensor("bup", [128, 16], F32, kind="ExternalInput").ap()
    bvp = nc.dram_tensor("bvp", [128, 16], F32, kind="ExternalInput").ap()
    bg2p = nc.dram_tensor("bg2p", [128, 16], F32, kind="ExternalInput").ap()
    bg1p = nc.dram_tensor("bg1p", [128, 4], F32, kind="ExternalInput").ap()
    cwp = nc.dram_tensor("cwp", [128, 48], F32, kind="ExternalInput").ap()
    cbp = nc.dram_tensor("cbp", [128, 16], F32, kind="ExternalInput").ap()
    tki = nc.dram_tensor("tki", [128, 64], I32, kind="ExternalInput").ap()
    tkw = nc.dram_tensor("tkw", [128, 64], F32, kind="ExternalInput").ap()
    tokid = nc.dram_tensor("tokid", [128, 64], F32, kind="ExternalInput").ap()
    srowid = nc.dram_tensor("srowid", [128, 64], F32, kind="ExternalInput").ap()
    ltri = nc.dram_tensor("ltri", [128, 128], F32, kind="ExternalInput").ap()

    outt = nc.dram_tensor("outt", [D, T], F32, kind="ExternalOutput").ap()
    conv_h = nc.dram_tensor("conv_h", [128, NDC * T], BF16, kind="Internal").ap()
    xn_fm = nc.dram_tensor("xn_fm", [D, T], BF16, kind="Internal").ap()
    xn_tm = nc.dram_tensor("xn_tm", [T, D], BF16,
                           kind="ExternalOutput" if _DBG else "Internal").ap()
    xg_h = nc.dram_tensor("xg_h", [NE * CAP, D], BF16,
                          kind="ExternalOutput" if _DBG else "Internal").ap()
    o_h = nc.dram_tensor("o_h", [NE, D, CAP], BF16,
                         kind="ExternalOutput" if _DBG else "Internal").ap()
    y4 = nc.dram_tensor("y4", [NSLOT + 128, D], BF16,
                        kind="ExternalOutput" if _DBG else "Internal").ap()
    glist = nc.dram_tensor("glist", [NE * CAP, 4], F32,
                           kind="ExternalOutput" if _DBG else "Internal").ap()
    if _DBG:
        d_xn = nc.dram_tensor("d_xn", [128, NDC * T], BF16,
                              kind="ExternalOutput").ap()
        d_y1 = nc.dram_tensor("d_y1", [128, NDC * T], BF16,
                              kind="ExternalOutput").ap()
        d_go = nc.dram_tensor("d_go", [128, NRC * T], BF16,
                              kind="ExternalOutput").ap()

    lists = {e: [] for e in ENGINES}
    cnt = {s: 0 for s in ("sw", "sx", "sq", "st", "sv", "ss", "so")}
    waited = {e: {} for e in ENGINES}
    semh = {}

    def op(eng, builder, waits=(), inc=None, amt=1):
        need = tuple((s, v) for (s, v) in waits
                     if v and v > waited[eng].get(s, 0))
        for s, v in need:
            waited[eng][s] = v
        val = None
        if inc is not None:
            cnt[inc] += amt
            val = cnt[inc]

        def f(e, _n=need, _b=builder, _i=inc, _a=amt):
            for s, v in _n:
                e.wait_ge(semh[s], v)
            if _b is not None:
                r = _b(e)
                if _i is not None:
                    r.then_inc(semh[_i], _a)
        lists[eng].append(f)
        return val

    def wait_only(eng, waits):
        op(eng, None, waits=waits)

    def barrier_all():
        b = tuple(cnt.items())
        for e in ENGINES:
            wait_only(e, b)

    def flush_block(name):
        L = {e: lists[e] for e in ENGINES}
        for e in ENGINES:
            lists[e] = []
        with nc.Block(name) as blk:
            @blk.sync
            def _(e):
                for f in L["sync"]:
                    f(e)

            @blk.tensor
            def _(e):
                for f in L["tensor"]:
                    f(e)

            @blk.vector
            def _(e):
                for f in L["vector"]:
                    f(e)

            @blk.scalar
            def _(e):
                for f in L["scalar"]:
                    f(e)

            @blk.gpsimd
            def _(e):
                for f in L["gpsimd"]:
                    f(e)

    with contextlib.ExitStack() as stack:
        ent = stack.enter_context

        for s in ("sw", "sx", "sq", "st", "sv", "ss", "so"):
            semh[s] = ent(nc.semaphore(s))

        xn = ent(nc.sbuf_tensor("xn", [128, NDC * T], BF16))
        ones128 = ent(nc.sbuf_tensor("ones128", [128, 1], BF16))
        ones1f = ent(nc.sbuf_tensor("ones1f", [1, 128], F32))
        zcol = ent(nc.sbuf_tensor("zcol", [128, 1], F32))
        bup_sb = ent(nc.sbuf_tensor("bup_sb", [128, 16], F32))
        bvp_sb = ent(nc.sbuf_tensor("bvp_sb", [128, 16], F32))
        bg2_sb = ent(nc.sbuf_tensor("bg2_sb", [128, 16], F32))
        bg1_sb = ent(nc.sbuf_tensor("bg1_sb", [128, 4], F32))
        cwp_sb = ent(nc.sbuf_tensor("cwp_sb", [128, 48], F32))
        cbp_sb = ent(nc.sbuf_tensor("cbp_sb", [128, 16], F32))
        epsc = ent(nc.sbuf_tensor("epsc", [1, 1], F32))
        ltri_sb = ent(nc.sbuf_tensor("ltri_sb", [128, 128], F32))

        psA = ent(nc.psum_tensor("psA", [128, 2048], F32))
        psB = ent(nc.psum_tensor("psB", [128, 2048], F32))
        ps_of = {0: psA, 1: psB}
        bank = [(psA, q) for q in range(4)] + [(psB, q) for q in range(4)]

        def bank_ap(i):
            t, q = bank[i]
            return t[:, q * TB:(q + 1) * TB]

        with (
            nc.sbuf_tensor("ybuf", [128, NDC * T], BF16) as ybuf,
            nc.sbuf_tensor("idxi", [128, 64], I32) as idxi,
            nc.sbuf_tensor("tkw_sb", [128, 64], F32) as tkw_sb,
            nc.sbuf_tensor("tok_sb", [128, 64], F32) as tok_sb,
            nc.sbuf_tensor("srw_sb", [128, 64], F32) as srw_sb,
            nc.sbuf_tensor("padt", [128, 512], F32) as padt,
        ):
            ln_scope = contextlib.ExitStack()
            x_stage = ln_scope.enter_context(
                nc.sbuf_tensor("x_stage", [128, 2 * T], F32))
            xbft = ln_scope.enter_context(
                nc.sbuf_tensor("xbft", [128, T], BF16))
            sqbf = ln_scope.enter_context(
                nc.sbuf_tensor("sqbf", [128, T], BF16))
            meanbc = ln_scope.enter_context(
                nc.sbuf_tensor("meanbc", [128, T], F32))
            rstdbc = ln_scope.enter_context(
                nc.sbuf_tensor("rstdbc", [128, T], F32))
            r1 = ln_scope.enter_context(nc.sbuf_tensor("r1", [1, T], F32))
            r2 = ln_scope.enter_context(nc.sbuf_tensor("r2", [1, T], F32))
            r3 = ln_scope.enter_context(nc.sbuf_tensor("r3", [1, T], F32))
            tmp1 = ln_scope.enter_context(
                nc.sbuf_tensor("tmp1", [128, T], F32))
            op("gpsimd", lambda e: e.memset(ones128[:, :], 1.0))
            op("gpsimd", lambda e: e.memset(ones1f[:, :], 1.0))
            op("gpsimd", lambda e: e.memset(zcol[:, :], 0.0))
            op("gpsimd", lambda e: e.memset(epsc[:, :], 1e-5))
            m_pad = op("gpsimd", lambda e: e.memset(padt[:, :], PAD),
                       inc="sw", amt=16)
            m_pad2 = op(
                "vector",
                lambda e: e.tensor_scalar_mul(
                    padt[:, :].rearrange("p (a v) -> p a v", v=4)[:, :, 0:1],
                    padt[:, :].rearrange("p (a v) -> p a v", v=4)[:, :, 0:1],
                    0.0),
                waits=[("sw", m_pad)], inc="sv")
            gl_init = op(
                "gpsimd",
                lambda e: e.dma_start(
                    glist.rearrange("(p a) v -> p (a v)", p=128),
                    padt[:, :]),
                waits=[("sw", m_pad), ("sv", m_pad2)], inc="sw", amt=16)
            for dst, src in ((bup_sb, bup), (bvp_sb, bvp), (bg2_sb, bg2p),
                             (bg1_sb, bg1p), (cwp_sb, cwp), (cbp_sb, cbp),
                             (ltri_sb, ltri), (tkw_sb, tkw), (tok_sb, tokid),
                             (srw_sb, srowid), (idxi, tki)):
                op("sync", lambda e, d=dst, s=src: e.dma_start(d[:, :], s),
                   inc="sx", amt=16)
            sx_small = cnt["sx"]

            def xdma_op(dc, sl, w):
                return op(
                    "sync",
                    lambda e, dc=dc, sl=sl: e.dma_start(
                        x_stage[:, sl:sl + T], xt[dc * 128:(dc + 1) * 128, :]),
                    waits=w, inc="sx", amt=16)

            xdma = {0: xdma_op(0, 0, []), 1: xdma_op(1, T, [])}
            cvt_done = {}
            mm_done = {}
            for dc in range(NDC):
                sl = (dc % 2) * T
                w = [("sx", xdma[dc])]
                if dc >= 1:
                    w.append(("st", mm_done[dc - 1]))
                op("scalar",
                   lambda e, sl=sl: e.copy(xbft[:, :], x_stage[:, sl:sl + T]),
                   waits=w)
                cvt_done[dc] = op(
                    "scalar",
                    lambda e, sl=sl: e.activation(
                        sqbf[:, :], x_stage[:, sl:sl + T], AF.Square),
                    inc="ss")
                for tb in range(NTB):
                    op("tensor",
                       lambda e, dc=dc, tb=tb: e.matmul(
                           psA[0:1, tb * TB:(tb + 1) * TB], ones128[:, :],
                           xbft[:, tb * TB:(tb + 1) * TB],
                           start=(dc == 0), stop=(dc == NDC - 1)),
                       waits=[("ss", cvt_done[dc])])
                for tb in range(NTB):
                    mm_done[dc] = op(
                        "tensor",
                        lambda e, dc=dc, tb=tb: e.matmul(
                            psB[0:1, tb * TB:(tb + 1) * TB], ones128[:, :],
                            sqbf[:, tb * TB:(tb + 1) * TB],
                            start=(dc == 0), stop=(dc == NDC - 1)),
                        inc="st" if tb == NTB - 1 else None)
                if dc + 2 < NDC:
                    xdma[dc + 2] = xdma_op(dc + 2, (dc % 2) * T,
                                           [("ss", cvt_done[dc])])

            stats_done = mm_done[NDC - 1]
            inv = 1.0 / D
            v_mean = op("vector",
                        lambda e: e.tensor_scalar_mul(r1[:, :], psA[0:1, :],
                                                      inv),
                        waits=[("st", stats_done)], inc="sv")
            op("vector", lambda e: e.tensor_mul(r2[:, :], r1[:, :], r1[:, :]))
            v_var = op("vector",
                       lambda e: e.scalar_tensor_tensor(
                           r2[:, :], psB[0:1, :], inv, r2[:, :],
                           ALU.mult, ALU.subtract),
                       inc="sv")
            s_sq = op("scalar",
                      lambda e: e.activation(r3[:, :], r2[:, :], AF.Sqrt,
                                             bias=epsc[0:1, :]),
                      waits=[("sv", v_var)], inc="ss")
            v_rec = op("vector", lambda e: e.reciprocal(r2[:, :], r3[:, :]),
                       waits=[("ss", s_sq)], inc="sv")
            bc_done = None
            for tb in range(NTB):
                op("tensor",
                   lambda e, tb=tb: e.matmul(
                       psA[:, tb * TB:(tb + 1) * TB], ones1f[:, :],
                       r1[0:1, tb * TB:(tb + 1) * TB], start=True, stop=True),
                   waits=[("sv", v_rec)])
            for tb in range(NTB):
                bc_done = op(
                    "tensor",
                    lambda e, tb=tb: e.matmul(
                        psB[:, tb * TB:(tb + 1) * TB], ones1f[:, :],
                        r2[0:1, tb * TB:(tb + 1) * TB], start=True, stop=True),
                    inc="st" if tb == NTB - 1 else None)
            s_mean = op("scalar", lambda e: e.copy(meanbc[:, :], psA[:, :]),
                        waits=[("st", bc_done)], inc="ss")
            v_rstd = op("vector", lambda e: e.tensor_copy(rstdbc[:, :],
                                                          psB[:, :]),
                        waits=[("st", bc_done)], inc="sv")

            w0 = [("ss", s_mean), ("sv", v_rstd)]
            xdma2 = {0: xdma_op(0, 0, w0), 1: xdma_op(1, T, w0)}
            bdone = {}
            edge_done = {}
            for dc in range(NDC):
                sl = (dc % 2) * T
                w = [("sx", xdma2[dc]), ("ss", s_mean), ("sv", v_rstd)]
                if dc >= 1:
                    w.append(("ss", edge_done[dc - 1]))
                op("vector",
                   lambda e, sl=sl: e.tensor_sub(
                       tmp1[:, :], x_stage[:, sl:sl + T], meanbc[:, :]),
                   waits=w)
                xn_dc = op(
                    "vector",
                    lambda e, dc=dc: e.tensor_mul(
                        xn[:, dc * T:(dc + 1) * T], tmp1[:, :], rstdbc[:, :]),
                    inc="sv")
                s_c1 = op("scalar",
                          lambda e, dc=dc: e.activation(
                              tmp1[:, :], xn[:, dc * T:(dc + 1) * T],
                              AF.Identity,
                              bias=cbp_sb[:, dc:dc + 1],
                              scale=cwp_sb[:, dc * 3 + 1:dc * 3 + 2]),
                          waits=[("sv", xn_dc), ("sx", sx_small)], inc="ss")
                op("vector",
                   lambda e, dc=dc: e.scalar_tensor_tensor(
                       tmp1[:, 1:T], xn[:, dc * T:dc * T + T - 1],
                       cwp_sb[:, dc * 3:dc * 3 + 1], tmp1[:, 1:T],
                       ALU.mult, ALU.add),
                   waits=[("ss", s_c1)])
                bdone[dc] = op(
                    "vector",
                    lambda e, dc=dc: e.scalar_tensor_tensor(
                        ybuf[:, dc * T:dc * T + T - 1],
                        xn[:, dc * T + 1:(dc + 1) * T],
                        cwp_sb[:, dc * 3 + 2:dc * 3 + 3], tmp1[:, 0:T - 1],
                        ALU.mult, ALU.add),
                    inc="sv")
                edge_done[dc] = op(
                    "scalar",
                    lambda e, dc=dc: e.copy(
                        ybuf[:, dc * T + T - 1:(dc + 1) * T],
                        tmp1[:, T - 1:T]),
                    waits=[("sv", bdone[dc])], inc="ss")
                if dc + 2 < NDC:
                    xdma2[dc + 2] = xdma_op(dc + 2, (dc % 2) * T,
                                            [("sv", bdone[dc])])
            xn_ready = cnt["sv"]
            convs_done = cnt["ss"]

            ln_scope.close()
            rt_scope = contextlib.ExitStack()
            ttile = rt_scope.enter_context(
                nc.sbuf_tensor("ttile", [128, 2 * T], BF16))
            idxf = rt_scope.enter_context(
                nc.sbuf_tensor("idxf", [128, 64], F32))
            macc = rt_scope.enter_context(
                nc.sbuf_tensor("macc", [128, 64], F32))
            mmask = rt_scope.enter_context(
                nc.sbuf_tensor("mmask", [128, 64], F32))
            mincl = rt_scope.enter_context(
                nc.sbuf_tensor("mincl", [128, 64], F32))
            mpos = rt_scope.enter_context(
                nc.sbuf_tensor("mpos", [128, 64], F32))
            mtmp = rt_scope.enter_context(
                nc.sbuf_tensor("mtmp", [128, 64], F32))
            ctot = rt_scope.enter_context(
                nc.sbuf_tensor("ctot", [128, 1], F32))
            addr_i = rt_scope.enter_context(
                nc.sbuf_tensor("addr_i", [128, 64], I32))
            val = rt_scope.enter_context(
                nc.sbuf_tensor("val", [128, 256], F32))

            conv_sp = op("sync", lambda e: e.dma_start(conv_h, ybuf[:, :]),
                         waits=[("sv", xn_ready), ("ss", convs_done)],
                         inc="sx", amt=16)

            xnfm_sp = op(
                "sync",
                lambda e: e.dma_start(
                    xn_fm.rearrange("(c p) t -> p c t", p=128),
                    xn[:, :].rearrange("p (c t) -> p c t", c=NDC)),
                waits=[("sv", xn_ready)], inc="sx", amt=16)
            tl_done = {}
            st_done = {}
            for tc in range(NDC):
                sl = (tc % 2) * T
                w = [("sx", xnfm_sp)]
                if tc >= 2:
                    w.append(("sx", st_done[tc - 2]))
                tl_done[tc] = op(
                    "scalar",
                    lambda e, tc=tc, sl=sl: e.dma_start_transpose(
                        ttile[:, sl:sl + T],
                        xn_fm[:, tc * 128:(tc + 1) * 128]),
                    waits=w, inc="sq", amt=16)
                st_done[tc] = op(
                    "sync",
                    lambda e, tc=tc, sl=sl: e.dma_start(
                        xn_tm[tc * 128:(tc + 1) * 128, :],
                        ttile[:, sl:sl + T]),
                    waits=[("sq", tl_done[tc])], inc="sx", amt=16)
            xn_tm_ready = st_done[NDC - 1]

            v_idx = op("vector", lambda e: e.tensor_copy(idxf[:, :],
                                                         idxi[:, :]),
                       waits=[("sx", sx_small), ("ss", convs_done),
                              ("sv", xn_ready)], inc="sv")
            op("vector", lambda e: e.memset(macc[:, :], 0.0))
            for e_i in range(NE):
                op("vector",
                   lambda e, e_i=e_i: e.tensor_single_scalar(
                       mmask[:, :], idxf[:, :], float(e_i), ALU.is_equal),
                   waits=[("sv", v_idx)])
                v_incl = op("vector",
                            lambda e: e.tensor_tensor_scan(
                                mincl[:, :], mmask[:, :],
                                zcol[:, :].broadcast_to((128, 64)), 0.0,
                                ALU.add, ALU.add),
                            inc="sv")
                rt_mm = op("tensor",
                           lambda e: e.matmul(psA[:, 0:1], ltri_sb[:, :],
                                              mincl[:, 63:64],
                                              start=True, stop=True),
                           waits=[("sv", v_incl)],
                           inc="st")
                s_ct = op("scalar",
                          lambda e: e.copy(ctot[:, :], psA[:, 0:1]),
                          waits=[("st", rt_mm)], inc="ss")
                op("vector",
                   lambda e: e.tensor_sub(mpos[:, :], mincl[:, :],
                                          mmask[:, :]))
                op("vector",
                   lambda e, e_i=e_i: e.tensor_scalar(
                       mpos[:, :], mpos[:, :], ctot[:, :], float(e_i * CAP),
                       ALU.add, ALU.add),
                   waits=[("ss", s_ct)])
                op("vector",
                   lambda e: e.tensor_mul(mtmp[:, :], mmask[:, :],
                                          mpos[:, :]))
                op("vector",
                   lambda e: e.tensor_add(macc[:, :], macc[:, :],
                                          mtmp[:, :]))
            v_addr = op("vector", lambda e: e.tensor_copy(addr_i[:, :],
                                                          macc[:, :]),
                        inc="sv")
            vz = op("vector", lambda e: e.memset(val[:, :], 0.0), inc="sv")
            vlast = vz
            for c, srcv in ((0, tok_sb), (1, srw_sb), (2, tkw_sb)):
                vlast = op(
                    "vector",
                    lambda e, c=c, srcv=srcv: e.tensor_copy(
                        val[:, :].rearrange(
                            "p (j c) -> p j c", c=4)[:, :, c:c + 1],
                        srcv[:, :].rearrange("p (j o) -> p j o", o=1)),
                    waits=[("sv", vz), ("sx", sx_small)], inc="sv")
            glsc = 0
            for j in range(64):
                glsc = op(
                    "gpsimd",
                    lambda e, j=j: e.indirect_dma_start(
                        glist,
                        bass.IndirectOffsetOnAxis(ap=addr_i[:, j:j + 1],
                                                  axis=0),
                        val[:, 4 * j:4 * j + 4], None),
                    waits=[("sv", v_addr), ("sv", vlast), ("sw", gl_init)],
                    inc="sw", amt=16)
            glist_ready = glsc
            if _DBG:
                op("sync", lambda e: e.dma_start(d_xn, xn[:, :]),
                   waits=[("sv", xn_ready)], inc="so", amt=16)
                op("sync", lambda e: e.dma_start(d_y1, ybuf[:, :]),
                   waits=[("ss", convs_done)], inc="so", amt=16)
            flush_block("ln")
            rt_scope.close()

        barrier_all()
        with nc.sbuf_tensor("wring", [128, 4 * 2048], BF16) as wring:
            slot_free = [("st", 0)] * 4
            blk_i = [0]

            def wblock_dma(src_ap, waits=()):
                j = blk_i[0] % 4
                blk_i[0] += 1
                dst = wring[:, j * 2048: j * 2048 + src_ap.shape[-1]]
                v = op("gpsimd",
                       lambda e, dst=dst, src=src_ap: e.dma_start(dst, src),
                       waits=[slot_free[j]] + list(waits), inc="sw", amt=16)
                return v, j

            def wflat_dma(row, off, width):
                j = blk_i[0] % 4
                blk_i[0] += 1
                dst = wring[:, j * 2048: j * 2048 + width]
                src = wmoe[row, off: off + 128 * width].rearrange(
                    "(p w) -> p w", w=width)
                v = op("gpsimd",
                       lambda e, dst=dst, src=src: e.dma_start(dst, src),
                       waits=[slot_free[j]], inc="sw", amt=16)
                return v, j

            with (
                nc.sbuf_tensor("glt", [128, 2 * 16], F32) as glt,
                nc.sbuf_tensor("gidx", [128, 2 * NCT], I32) as gidx,
                nc.sbuf_tensor("srow", [128, 2 * NCT], I32) as srow,
                nc.sbuf_tensor("xgst", [128, 2 * D], BF16) as xgst,
                nc.sbuf_tensor("xg_fm", [128, 2 * 16 * TB], BF16) as xg_fm,
                nc.sbuf_tensor("hmid", [128, NIC * TB], BF16) as hmid,
                nc.sbuf_tensor("silu_t", [128, 2 * TB], BF16) as silu_t,
                nc.sbuf_tensor("o_fm", [128, 2 * 16 * TB], BF16) as o_fm,
                nc.sbuf_tensor("o_tm", [128, 2 * D], BF16) as o_tm,
                nc.sbuf_tensor("o_sc", [128, 2 * D], BF16) as o_sc,
            ):
                z0 = op("gpsimd", lambda e: e.memset(xgst[:, :], 0.0),
                        inc="sw", amt=16)

                glt_l = {}
                drv_done = {}
                gst_done = {}
                tl16 = {}
                ofm_rd = {}
                otl = {}
                osc_done = {}
                sc_done = {}
                xg_used = {}
                hmid_used = {}
                ofm_evac = {}
                glt_used = {}

                def emit_gather(e_i):
                    par = e_i % 2
                    w = [("sw", glist_ready)]
                    if e_i >= 2:
                        w.append(("sv", glt_used.get(e_i - 2, 0)))
                    glt_l[e_i] = op(
                        "sync",
                        lambda e, e_i=e_i, par=par: e.dma_start(
                            glt[:, par * 16:(par + 1) * 16].rearrange(
                                "p (ct v) -> p ct v", v=4),
                            glist[e_i * CAP:(e_i + 1) * CAP, :].rearrange(
                                "(ct p) v -> p ct v", p=128)),
                        waits=w, inc="sx", amt=16)
                    g3 = glt[:, par * 16:(par + 1) * 16].rearrange(
                        "p (ct v) -> p ct v", v=4)
                    wv = [("sx", glt_l[e_i])]
                    if (e_i - 2, NCT - 1) in gst_done:
                        wv.append(("sw", gst_done[(e_i - 2, NCT - 1)]))
                    if (e_i - 2, NCT - 1) in sc_done:
                        wv.append(("sw", sc_done[(e_i - 2, NCT - 1)]))
                    op("vector",
                       lambda e, g3=g3, par=par: e.tensor_copy(
                           gidx[:, par * NCT:(par + 1) * NCT].rearrange(
                               "p (ct o) -> p ct o", o=1),
                           g3[:, :, 0:1]),
                       waits=wv)
                    drv_done[e_i] = op(
                        "vector",
                        lambda e, g3=g3, par=par: e.tensor_copy(
                            srow[:, par * NCT:(par + 1) * NCT].rearrange(
                                "p (ct o) -> p ct o", o=1),
                            g3[:, :, 1:2]),
                        inc="sv")
                    for ct in range(NCT):
                        sl = (ct % 2) * D
                        w = [("sv", drv_done[e_i]), ("sw", z0),
                             ("sx", xn_tm_ready)]
                        prev = (e_i, ct - 2) if ct >= 2 else (e_i - 1, ct + 2)
                        if prev in gst_done:
                            w.append(("sw", gst_done[prev]))
                        gg = op(
                            "gpsimd",
                            lambda e, sl=sl, par=par, ct=ct:
                            e.indirect_dma_start(
                                xgst[:, sl:sl + D], None,
                                xn_tm,
                                bass.IndirectOffsetOnAxis(
                                    ap=gidx[:, par * NCT + ct:
                                            par * NCT + ct + 1], axis=0)),
                            waits=w, inc="sw", amt=16)
                        gst_done[(e_i, ct)] = op(
                            "gpsimd",
                            lambda e, e_i=e_i, ct=ct, sl=sl: e.dma_start(
                                xg_h[e_i * CAP + ct * 128:
                                     e_i * CAP + (ct + 1) * 128, :],
                                xgst[:, sl:sl + D]),
                            waits=[("sw", gg)], inc="sw", amt=16)

                def emit_tloads(e_i):
                    par = e_i % 2
                    w = [("sw", gst_done[(e_i, NCT - 1)])]
                    if e_i >= 2:
                        w.append(("st", xg_used[e_i - 2]))
                    for hc in range(NDC):
                        tl16[e_i] = op(
                            "sync",
                            lambda e, e_i=e_i, hc=hc, par=par:
                            e.dma_start_transpose(
                                xg_fm[:, par * 16 * TB + hc * TB:
                                      par * 16 * TB + (hc + 1) * TB],
                                xg_h[e_i * CAP:(e_i + 1) * CAP,
                                     hc * 128:(hc + 1) * 128]),
                            waits=w, inc="sx", amt=16)

                emit_gather(0)
                emit_tloads(0)
                emit_gather(1)

                bank_free = [("sv", 0)] * 8
                bslot = [0]

                def next_bank(extra_waits=()):
                    i = bslot[0] % 8
                    bslot[0] += 1
                    return i, [bank_free[i]] + list(extra_waits)

                for e_i in range(NE):
                    par = e_i % 2
                    if e_i + 1 < NE:
                        emit_tloads(e_i + 1)
                    g_of = par * 16 * TB
                    silu_done = {}
                    hmid_done = {}
                    for ic in range(NIC):
                        wg, jg = wflat_dma(e_i, ic * 2 * GU, 2048)
                        wu_, ju = wflat_dma(e_i, ic * 2 * GU + GU, 2048)
                        bi, bw = next_bank([("sw", wg), ("sx", tl16[e_i])])
                        g_last = None
                        for hc in range(NDC):
                            g_last = op(
                                "tensor",
                                lambda e, jg=jg, hc=hc, bi=bi, g_of=g_of:
                                e.matmul(
                                    bank_ap(bi),
                                    wring[:, jg * 2048 + hc * 128:
                                          jg * 2048 + (hc + 1) * 128],
                                    xg_fm[:, g_of + hc * TB:
                                          g_of + (hc + 1) * TB],
                                    start=(hc == 0), stop=(hc == NDC - 1)),
                                waits=bw,
                                inc="st" if hc == NDC - 1 else None)
                        slot_free[jg] = ("st", g_last)
                        sp = (ic % 2) * TB
                        silu_done[ic] = op(
                            "scalar",
                            lambda e, bi=bi, sp=sp: e.activation(
                                silu_t[:, sp:sp + TB], bank_ap(bi), AF.Silu),
                            waits=[("st", g_last),
                                   ("sv", hmid_done.get(ic - 2, 0))],
                            inc="ss")
                        bank_free[bi] = ("ss", silu_done[ic])
                        bi2, bw2 = next_bank([("sw", wu_)])
                        u_last = None
                        for hc in range(NDC):
                            u_last = op(
                                "tensor",
                                lambda e, ju=ju, hc=hc, bi2=bi2, g_of=g_of:
                                e.matmul(
                                    bank_ap(bi2),
                                    wring[:, ju * 2048 + hc * 128:
                                          ju * 2048 + (hc + 1) * 128],
                                    xg_fm[:, g_of + hc * TB:
                                          g_of + (hc + 1) * TB],
                                    start=(hc == 0), stop=(hc == NDC - 1)),
                                waits=bw2,
                                inc="st" if hc == NDC - 1 else None)
                        slot_free[ju] = ("st", u_last)
                        if ic == NIC - 1:
                            xg_used[e_i] = u_last
                        hmid_done[ic] = op(
                            "vector",
                            lambda e, bi2=bi2, ic=ic, sp=sp: e.tensor_mul(
                                hmid[:, ic * TB:(ic + 1) * TB],
                                bank_ap(bi2), silu_t[:, sp:sp + TB]),
                            waits=[("st", u_last), ("ss", silu_done[ic]),
                                   ("st", hmid_used.get(e_i - 1, 0))],
                            inc="sv")
                        bank_free[bi2] = ("sv", hmid_done[ic])
                    for hc in range(NDC):
                        wd, jd = wflat_dma(e_i, 16 * GU + hc * DD, 1024)
                        bi, bw = next_bank([("sw", wd)])
                        o_last = None
                        for ic in range(NIC):
                            o_last = op(
                                "tensor",
                                lambda e, jd=jd, ic=ic, bi=bi: e.matmul(
                                    bank_ap(bi),
                                    wring[:, jd * 2048 + ic * 128:
                                          jd * 2048 + (ic + 1) * 128],
                                    hmid[:, ic * TB:(ic + 1) * TB],
                                    start=(ic == 0), stop=(ic == NIC - 1)),
                                waits=bw + [("sv", hmid_done[ic])],
                                inc="st" if ic == NIC - 1 else None)
                        slot_free[jd] = ("st", o_last)
                        if hc == NDC - 1:
                            hmid_used[e_i] = o_last
                        w = [("st", o_last)]
                        if e_i >= 2:
                            w.append(("sx", ofm_rd[e_i - 2]))
                        ofm_evac[e_i] = op(
                            "scalar",
                            lambda e, bi=bi, hc=hc, g_of=g_of: e.copy(
                                o_fm[:, g_of + hc * TB:g_of + (hc + 1) * TB],
                                bank_ap(bi)),
                            waits=w, inc="ss")
                        bank_free[bi] = ("ss", ofm_evac[e_i])
                    ofm_rd[e_i] = op(
                        "sync",
                        lambda e, e_i=e_i, g_of=g_of: e.dma_start(
                            o_h[e_i].rearrange("(hc p) c -> p hc c", p=128),
                            o_fm[:, g_of:g_of + 16 * TB].rearrange(
                                "p (hc c) -> p hc c", hc=16)),
                        waits=[("ss", ofm_evac[e_i])], inc="sx", amt=16)
                    for ct in range(NCT):
                        sl = (ct % 2) * D
                        w = [("sx", ofm_rd[e_i])]
                        prev = (e_i, ct - 2) if ct >= 2 else (e_i - 1, ct + 2)
                        if prev in osc_done:
                            w.append(("sv", osc_done[prev]))
                        otl[(e_i, ct)] = op(
                            "scalar",
                            lambda e, e_i=e_i, ct=ct, sl=sl:
                            e.dma_start_transpose(
                                o_tm[:, sl:sl + D],
                                o_h[e_i][:, ct * 128:(ct + 1) * 128]),
                            waits=w, inc="sq", amt=16)
                        wsc = [("sq", otl[(e_i, ct)])]
                        if prev in sc_done:
                            wsc.append(("sw", sc_done[prev]))
                        wc = par * 16 + ct * 4 + 2
                        osc_done[(e_i, ct)] = op(
                            "vector",
                            lambda e, sl=sl, wc=wc: e.tensor_scalar_mul(
                                o_sc[:, sl:sl + D], o_tm[:, sl:sl + D],
                                glt[:, wc:wc + 1]),
                            waits=wsc, inc="sv")
                        if ct == NCT - 1:
                            glt_used[e_i] = osc_done[(e_i, ct)]
                        sc_done[(e_i, ct)] = op(
                            "gpsimd",
                            lambda e, sl=sl, par=par, ct=ct:
                            e.indirect_dma_start(
                                y4,
                                bass.IndirectOffsetOnAxis(
                                    ap=srow[:, par * NCT + ct:
                                            par * NCT + ct + 1], axis=0),
                                o_sc[:, sl:sl + D], None),
                            waits=[("sv", osc_done[(e_i, ct)])],
                            inc="sw", amt=16)
                    if e_i + 2 < NE:
                        emit_gather(e_i + 2)
                scatters_done = cnt["sw"]
                flush_block("moe")

            barrier_all()
            with nc.sbuf_tensor("gobuf", [128, NRC * T], BF16) as gobuf:
                at_scope = contextlib.ExitStack()
                wg1_bf = at_scope.enter_context(
                    nc.sbuf_tensor("wg1_bf", [128, 8192], BF16))
                h_all = at_scope.enter_context(
                    nc.sbuf_tensor("h_all", [128, NDHC * T], BF16))
                gates = at_scope.enter_context(
                    nc.sbuf_tensor("gates", [128, T], F32))
                ug = at_scope.enter_context(
                    nc.sbuf_tensor("ug", [128, T], F32))
                vg = at_scope.enter_context(
                    nc.sbuf_tensor("vg", [128, T], F32))
                d_wg1 = op("gpsimd", lambda e: e.dma_start(wg1_bf[:, :], wg1d),
                           inc="sw", amt=16)
                h_done = {}
                relu_done = {}
                for dhc in range(NDHC):
                    ps = ps_of[dhc % 2]
                    for dc in range(NDC):
                        for tb in range(NTB):
                            h_done[dhc] = op(
                                "tensor",
                                lambda e, ps=ps, dc=dc, dhc=dhc, tb=tb:
                                e.matmul(
                                    ps[:, tb * TB:(tb + 1) * TB],
                                    wg1_bf[:, dc * 512 + dhc * 128:
                                           dc * 512 + (dhc + 1) * 128],
                                    xn[:, dc * T + tb * TB:
                                       dc * T + (tb + 1) * TB],
                                    start=(dc == 0), stop=(dc == NDC - 1)),
                                waits=[("sw", d_wg1),
                                       ("ss", relu_done.get(dhc - 2, 0))],
                                inc="st" if (dc == NDC - 1 and tb == NTB - 1)
                                else None)
                    relu_done[dhc] = op(
                        "scalar",
                        lambda e, ps=ps, dhc=dhc: e.activation(
                            h_all[:, dhc * T:(dhc + 1) * T], ps[:, :], AF.Relu,
                            bias=bg1_sb[:, dhc:dhc + 1]),
                        waits=[("st", h_done[dhc])], inc="ss")
                h_ready = relu_done[NDHC - 1]

                psA_free = ("ss", relu_done.get(NDHC - 2, 0))
                psB_free = ("ss", h_ready)
                go_done = {}
                for rc in range(NRC):
                    v2, jg2 = wblock_dma(wg2d[rc])
                    vu, ju_ = wblock_dma(wud[rc])
                    vv, jv_ = wblock_dma(wvd[rc])

                    gmm = None
                    for dhc in range(NDHC):
                        for tb in range(NTB):
                            gmm = op(
                                "tensor",
                                lambda e, jg2=jg2, dhc=dhc, tb=tb: e.matmul(
                                    psB[:, tb * TB:(tb + 1) * TB],
                                    wring[:, jg2 * 2048 + dhc * 128:
                                          jg2 * 2048 + (dhc + 1) * 128],
                                    h_all[:, dhc * T + tb * TB:
                                          dhc * T + (tb + 1) * TB],
                                    start=(dhc == 0), stop=(dhc == NDHC - 1)),
                                waits=[("sw", v2), ("ss", h_ready), psB_free],
                                inc="st" if (dhc == NDHC - 1 and tb == NTB - 1)
                                else None)
                    slot_free[jg2] = ("st", gmm)
                    s_gate = op(
                        "scalar",
                        lambda e, rc=rc: e.activation(
                            gates[:, :], psB[:, :], AF.Sigmoid,
                            bias=bg2_sb[:, rc:rc + 1]),
                        waits=[("st", gmm)], inc="ss")
                    psB_free = ("ss", s_gate)
                    umm = None
                    for dc in range(NDC):
                        for tb in range(NTB):
                            umm = op(
                                "tensor",
                                lambda e, ju_=ju_, dc=dc, tb=tb: e.matmul(
                                    psA[:, tb * TB:(tb + 1) * TB],
                                    wring[:, ju_ * 2048 + dc * 128:
                                          ju_ * 2048 + (dc + 1) * 128],
                                    xn[:, dc * T + tb * TB:
                                       dc * T + (tb + 1) * TB],
                                    start=(dc == 0), stop=(dc == NDC - 1)),
                                waits=[("sw", vu), psA_free],
                                inc="st" if (dc == NDC - 1 and tb == NTB - 1)
                                else None)
                    slot_free[ju_] = ("st", umm)
                    v_ug = op("vector",
                              lambda e, rc=rc: e.scalar_tensor_tensor(
                                  ug[:, :], psA[:, :], bup_sb[:, rc:rc + 1],
                                  gates[:, :], ALU.add, ALU.mult),
                              waits=[("st", umm), ("ss", s_gate)], inc="sv")
                    psA_free = ("sv", v_ug)
                    vmm = None
                    for dc in range(NDC):
                        for tb in range(NTB):
                            vmm = op(
                                "tensor",
                                lambda e, jv_=jv_, dc=dc, tb=tb: e.matmul(
                                    psB[:, tb * TB:(tb + 1) * TB],
                                    wring[:, jv_ * 2048 + dc * 128:
                                          jv_ * 2048 + (dc + 1) * 128],
                                    xn[:, dc * T + tb * TB:
                                       dc * T + (tb + 1) * TB],
                                    start=(dc == 0), stop=(dc == NDC - 1)),
                                waits=[("sw", vv), psB_free],
                                inc="st" if (dc == NDC - 1 and tb == NTB - 1)
                                else None)
                    slot_free[jv_] = ("st", vmm)
                    v_vg = op("vector",
                              lambda e, rc=rc: e.scalar_tensor_tensor(
                                  vg[:, :], psB[:, :], bvp_sb[:, rc:rc + 1],
                                  gates[:, :], ALU.add, ALU.mult),
                              waits=[("st", vmm)], inc="sv")
                    psB_free = ("sv", v_vg)
                    op("vector",
                       lambda e: e.tensor_tensor_scan(
                           ug[:, :], ug[:, :],
                           zcol[:, :].broadcast_to((128, T)), 0.0,
                           ALU.add, ALU.add))
                    go_done[rc] = op(
                        "vector",
                        lambda e, rc=rc: e.tensor_mul(
                            gobuf[:, rc * T:(rc + 1) * T], ug[:, :],
                            vg[:, :]),
                        inc="sv")
                go_ready = go_done[NRC - 1]
                if _DBG:
                    op("sync", lambda e: e.dma_start(d_go, gobuf[:, :]),
                       waits=[("sv", go_ready)], inc="so", amt=16)
                flush_block("attn")
                at_scope.close()

                barrier_all()
                with (
                    nc.sbuf_tensor("x_stage2", [128, T], F32) as x_stage2,
                    nc.sbuf_tensor("cstr", [128, T], BF16) as cstr,
                    nc.sbuf_tensor("ytp", [128, NSLOT], BF16) as ytp,
                    nc.sbuf_tensor("ostg", [128, T], F32) as ostg,
                    nc.sbuf_tensor("ftmp", [128, T], F32) as ftmp,
                    nc.sbuf_tensor("ftm2", [128, T], F32) as ftm2,
                ):
                    xl = {}
                    cl = {}
                    yl = {}
                    out_done = {}
                    vsum = {}
                    for dc in range(NDC):
                        w = [] if dc < 1 else [("sv", vsum[dc - 1])]
                        if dc >= 1:
                            out_done[dc - 1] = op(
                                "sync",
                                lambda e, dc=dc: e.dma_start(
                                    outt[(dc - 1) * 128:dc * 128, :],
                                    ostg[:, :]),
                                waits=w, inc="so", amt=16)
                        xl[dc] = op("sync",
                                    lambda e, dc=dc: e.dma_start(
                                        x_stage2[:, :],
                                        xt[dc * 128:(dc + 1) * 128, :]),
                                    waits=w, inc="sx", amt=16)
                        cl[dc] = op("sync",
                                    lambda e, dc=dc: e.dma_start(
                                        cstr[:, :],
                                        conv_h[:, dc * T:(dc + 1) * T]),
                                    inc="sx", amt=16)
                        yl[dc] = op(
                            "scalar",
                            lambda e, dc=dc: e.dma_start_transpose(
                                ytp[:, :],
                                y4[0:NSLOT, dc * 128:(dc + 1) * 128]),
                            waits=(w + [("sw", scatters_done)]),
                            inc="sq", amt=16)
                        vw, jw = wblock_dma(wud[dc])
                        ps = ps_of[dc % 2]
                        uv_last = None
                        for rc in range(NRC):
                            for tb in range(NTB):
                                uv_last = op(
                                    "tensor",
                                    lambda e, ps=ps, jw=jw, rc=rc, tb=tb:
                                    e.matmul(
                                        ps[:, tb * TB:(tb + 1) * TB],
                                        wring[:, jw * 2048 + rc * 128:
                                              jw * 2048 + (rc + 1) * 128],
                                        gobuf[:, rc * T + tb * TB:
                                              rc * T + (tb + 1) * TB],
                                        start=(rc == 0),
                                        stop=(rc == NRC - 1)),
                                    waits=[("sw", vw),
                                           ("sv", vsum.get(dc - 2, 0))],
                                    inc="st" if (rc == NRC - 1 and
                                                 tb == NTB - 1) else None)
                        slot_free[jw] = ("st", uv_last)
                        op("vector",
                           lambda e, ps=ps: e.tensor_add(
                               ftmp[:, :], ps[:, :], x_stage2[:, :]),
                           waits=[("st", uv_last), ("sx", xl[dc]),
                                  ("sx", cl[dc]), ("sq", yl[dc]),
                                  ("so", out_done.get(dc - 1, 0))])
                        op("vector",
                           lambda e: e.tensor_add(
                               ftmp[:, :], ftmp[:, :], cstr[:, :]))
                        op("vector",
                           lambda e: e.tensor_add(
                               ftm2[:, :], ytp[:, 0:T], ytp[:, T:2 * T]))
                        op("vector",
                           lambda e: e.tensor_add(ftmp[:, :], ftmp[:, :],
                                                  ftm2[:, :]))
                        op("vector",
                           lambda e: e.tensor_add(
                               ftm2[:, :], ytp[:, 2 * T:3 * T],
                               ytp[:, 3 * T:4 * T]))
                        vsum[dc] = op(
                            "vector",
                            lambda e: e.tensor_add(
                                ostg[:, :], ftmp[:, :], ftm2[:, :]),
                            inc="sv")
                    out_done[NDC - 1] = op(
                        "sync",
                        lambda e: e.dma_start(
                            outt[(NDC - 1) * 128:NDC * 128, :], ostg[:, :]),
                        waits=[("sv", vsum[NDC - 1])], inc="so", amt=16)
                    wait_only("sync", [("so", cnt["so"])])
                    flush_block("final")

    return nc


def _pack_t4(a):
    n = a.shape[0] // 128
    c = a.shape[1] // 128
    return np.ascontiguousarray(
        a.reshape(n, 128, c, 128).transpose(2, 1, 0, 3).reshape(
            c, 128, n * 128))


def kernel(L0_attn_x, L0_attn_Wu, L0_attn_bu, L0_attn_Wv, L0_attn_bv,
           L0_attn_Wg1, L0_attn_bg1, L0_attn_Wg2, L0_attn_bg2,
           L0_attn_conv_w, L0_attn_conv_b,
           L0_ffn_gate_weight, L0_ffn_up_weight, L0_ffn_down_weight,
           L0_ffn_topk_idx, L0_ffn_topk_weight):
    x = np.asarray(L0_attn_x, dtype=np.float32)
    Wu = np.asarray(L0_attn_Wu, dtype=np.float32)
    Wv = np.asarray(L0_attn_Wv, dtype=np.float32)
    Wg1 = np.asarray(L0_attn_Wg1, dtype=np.float32)
    Wg2 = np.asarray(L0_attn_Wg2, dtype=np.float32)
    bu = np.asarray(L0_attn_bu, dtype=np.float32)
    bv = np.asarray(L0_attn_bv, dtype=np.float32)
    bg1 = np.asarray(L0_attn_bg1, dtype=np.float32)
    bg2 = np.asarray(L0_attn_bg2, dtype=np.float32)
    convw = np.asarray(L0_attn_conv_w, dtype=np.float32)
    convb = np.asarray(L0_attn_conv_b, dtype=np.float32)
    gw = np.asarray(L0_ffn_gate_weight, dtype=np.float32)
    uw = np.asarray(L0_ffn_up_weight, dtype=np.float32)
    dw = np.asarray(L0_ffn_down_weight, dtype=np.float32)
    tidx = np.asarray(L0_ffn_topk_idx, dtype=np.int32)
    tw = np.asarray(L0_ffn_topk_weight, dtype=np.float32)

    B = x.shape[0]

    wud = _pack_t4(np.ascontiguousarray(Wu.T))
    wvd = _pack_t4(np.ascontiguousarray(Wv.T))
    wg2d = _pack_t4(np.ascontiguousarray(Wg2.T))
    wg1d = np.ascontiguousarray(
        Wg1.T.reshape(16, 128, 512).transpose(1, 0, 2).reshape(128, 8192))
    moe_blocks = []
    for e in range(NE):
        gt = _pack_t4(np.ascontiguousarray(gw[e].T))
        ut = _pack_t4(np.ascontiguousarray(uw[e].T))
        dt = _pack_t4(np.ascontiguousarray(dw[e].T))
        parts = []
        for ic in range(NIC):
            parts.append(gt[ic].ravel())
            parts.append(ut[ic].ravel())
        for hc in range(NDC):
            parts.append(dt[hc].ravel())
        moe_blocks.append(np.concatenate(parts))
    wmoe_np = np.ascontiguousarray(np.stack(moe_blocks))
    del moe_blocks

    bup_np = np.ascontiguousarray(bu.reshape(16, 128).T)
    bvp_np = np.ascontiguousarray(bv.reshape(16, 128).T)
    bg2_np = np.ascontiguousarray(bg2.reshape(16, 128).T)
    bg1_np = np.ascontiguousarray(bg1.reshape(4, 128).T)
    cwp_np = np.ascontiguousarray(
        convw[:, 0, :].reshape(16, 128, 3).transpose(1, 0, 2).reshape(128, 48))
    cbp_np = np.ascontiguousarray(convb.reshape(16, 128).T)

    s_all = np.arange(NSLOT)
    t_all = s_all // NK
    k_all = s_all % NK
    tok_np = np.ascontiguousarray(t_all.astype(np.float32).reshape(128, 64))
    srow_np = np.ascontiguousarray(
        (k_all * T + t_all).astype(np.float32).reshape(128, 64))
    ltri_np = np.ascontiguousarray(np.triu(np.ones((128, 128), np.float32), 1))

    nc = _build_program()
    in_maps = []
    for b in range(B):
        ts = tidx[b * T:(b + 1) * T].reshape(128, 64)
        ws = tw[b * T:(b + 1) * T].reshape(128, 64)
        in_maps.append({
            "xt": np.ascontiguousarray(x[b].T),
            "wud": wud, "wvd": wvd, "wg2d": wg2d, "wg1d": wg1d,
            "wmoe": wmoe_np,
            "bup": bup_np, "bvp": bvp_np, "bg2p": bg2_np, "bg1p": bg1_np,
            "cwp": cwp_np, "cbp": cbp_np,
            "tki": np.ascontiguousarray(ts),
            "tkw": np.ascontiguousarray(ws),
            "tokid": tok_np, "srowid": srow_np, "ltri": ltri_np,
        })
    res = run_bass_kernel_spmd(nc, in_maps, core_ids=list(range(B)))
    if _DBG:
        kernel._dbg = res.results
    out = np.empty((B, T, D), dtype=np.float32)
    for b in range(B):
        out[b] = np.asarray(res.results[b]["outt"]).T
    return out


# version 10
# speedup vs baseline: 1.5901x; 1.5901x over previous
import contextlib

import numpy as np
import concourse.bass as bass
import concourse.mybir as mybir
from concourse.bass_utils import run_bass_kernel_spmd

F32 = mybir.dt.float32
BF16 = mybir.dt.bfloat16
I32 = mybir.dt.int32
AF = mybir.ActivationFunctionType
ALU = mybir.AluOpType

D = 2048
T = 2048
R = 2048
DH = 512
DI = 1024
NE = 32
NK = 4
CAP = 384
NCT = CAP // 128
NDC = D // 128
NRC = R // 128
NIC = DI // 128
NDHC = DH // 128
TB = 512
NTB = T // TB
GU = 128 * 2048
DD = 128 * 1024
EXP_F = 16 * GU + 16 * DD
NSLOT = T * NK
PAD = 8192.0

ENGINES = ("sync", "tensor", "vector", "scalar", "gpsimd")
SEMS = ("sw", "sx", "sy", "st", "sv", "ss", "sg", "so")


def _build_program():
    nc = bass.Bass("TRN2", target_bir_lowering=False)

    xt = nc.dram_tensor("xt", [D, T], F32, kind="ExternalInput").ap()
    wud = nc.dram_tensor("wud", [16, 128, 2048], F32, kind="ExternalInput").ap()
    wvd = nc.dram_tensor("wvd", [16, 128, 2048], F32, kind="ExternalInput").ap()
    wg2d = nc.dram_tensor("wg2d", [16, 128, 512], F32, kind="ExternalInput").ap()
    wg1d = nc.dram_tensor("wg1d", [128, 8192], F32, kind="ExternalInput").ap()
    wmoe = nc.dram_tensor("wmoe", [NE, EXP_F], F32, kind="ExternalInput").ap()
    bup = nc.dram_tensor("bup", [128, 16], F32, kind="ExternalInput").ap()
    bvp = nc.dram_tensor("bvp", [128, 16], F32, kind="ExternalInput").ap()
    bg2p = nc.dram_tensor("bg2p", [128, 16], F32, kind="ExternalInput").ap()
    bg1p = nc.dram_tensor("bg1p", [128, 4], F32, kind="ExternalInput").ap()
    cwp = nc.dram_tensor("cwp", [128, 48], F32, kind="ExternalInput").ap()
    cbp = nc.dram_tensor("cbp", [128, 16], F32, kind="ExternalInput").ap()
    tki = nc.dram_tensor("tki", [128, 64], I32, kind="ExternalInput").ap()
    tkw = nc.dram_tensor("tkw", [128, 64], F32, kind="ExternalInput").ap()
    tokid = nc.dram_tensor("tokid", [128, 64], F32, kind="ExternalInput").ap()
    srowid = nc.dram_tensor("srowid", [128, 64], F32, kind="ExternalInput").ap()
    ltri = nc.dram_tensor("ltri", [128, 128], F32, kind="ExternalInput").ap()
    idt = nc.dram_tensor("idt", [128, 128], F32, kind="ExternalInput").ap()

    outt = nc.dram_tensor("outt", [D, T], F32, kind="ExternalOutput").ap()
    conv_h = nc.dram_tensor("conv_h", [128, NDC * T], BF16, kind="Internal").ap()
    xn_tm = nc.dram_tensor("xn_tm", [T, D], BF16, kind="Internal").ap()
    y4 = nc.dram_tensor("y4", [NSLOT + 128, D], BF16, kind="Internal").ap()
    glist = nc.dram_tensor("glist", [NE * CAP, 4], F32, kind="Internal").ap()
    go_h = nc.dram_tensor("go_h", [R, T], BF16, kind="Internal").ap()

    lists = {e: [] for e in ENGINES}
    cnt = {s: 0 for s in SEMS}
    waited = {e: {} for e in ENGINES}
    semh = {}

    def op(eng, builder, waits=(), inc=None, amt=1):
        need = tuple((s, v) for (s, v) in waits
                     if v and v > waited[eng].get(s, 0))
        for s, v in need:
            waited[eng][s] = v
        val = None
        if inc is not None:
            cnt[inc] += amt
            val = cnt[inc]

        def f(e, _n=need, _b=builder, _i=inc, _a=amt):
            for s, v in _n:
                e.wait_ge(semh[s], v)
            if _b is not None:
                r = _b(e)
                if _i is not None:
                    r.then_inc(semh[_i], _a)
        lists[eng].append(f)
        return val

    def wait_only(eng, waits):
        op(eng, None, waits=waits)

    def barrier_all():
        b = tuple(cnt.items())
        for e in ENGINES:
            wait_only(e, b)

    def flush_block(name):
        L = {e: lists[e] for e in ENGINES}
        for e in ENGINES:
            lists[e] = []
        with nc.Block(name) as blk:
            @blk.sync
            def _(e):
                for f in L["sync"]:
                    f(e)

            @blk.tensor
            def _(e):
                for f in L["tensor"]:
                    f(e)

            @blk.vector
            def _(e):
                for f in L["vector"]:
                    f(e)

            @blk.scalar
            def _(e):
                for f in L["scalar"]:
                    f(e)

            @blk.gpsimd
            def _(e):
                for f in L["gpsimd"]:
                    f(e)

    with contextlib.ExitStack() as stack:
        ent = stack.enter_context

        for s in SEMS:
            semh[s] = ent(nc.semaphore(s))

        ones128 = ent(nc.sbuf_tensor("ones128", [128, 1], BF16))
        ones1f = ent(nc.sbuf_tensor("ones1f", [1, 128], F32))
        zcol = ent(nc.sbuf_tensor("zcol", [128, 1], F32))
        bup_sb = ent(nc.sbuf_tensor("bup_sb", [128, 16], F32))
        bvp_sb = ent(nc.sbuf_tensor("bvp_sb", [128, 16], F32))
        bg2_sb = ent(nc.sbuf_tensor("bg2_sb", [128, 16], F32))
        bg1_sb = ent(nc.sbuf_tensor("bg1_sb", [128, 4], F32))
        cwp_sb = ent(nc.sbuf_tensor("cwp_sb", [128, 48], F32))
        cbp_sb = ent(nc.sbuf_tensor("cbp_sb", [128, 16], F32))
        epsc = ent(nc.sbuf_tensor("epsc", [1, 1], F32))
        ltri_sb = ent(nc.sbuf_tensor("ltri_sb", [128, 128], F32))
        idt_sb = ent(nc.sbuf_tensor("idt_sb", [128, 128], BF16))
        wring = ent(nc.sbuf_tensor("wring", [128, 4 * 2048], BF16))

        slot_free = [("st", 0)] * 4
        blk_i = [0]

        def wblock_dma(src_ap, waits=()):
            j = blk_i[0] % 4
            blk_i[0] += 1
            dst = wring[:, j * 2048: j * 2048 + src_ap.shape[-1]]
            v = op("gpsimd",
                   lambda e, dst=dst, src=src_ap: e.dma_start(dst, src),
                   waits=[slot_free[j]] + list(waits), inc="sw", amt=16)
            return v, j

        def wflat_dma(row, off, width):
            j = blk_i[0] % 4
            blk_i[0] += 1
            dst = wring[:, j * 2048: j * 2048 + width]
            src = wmoe[row, off: off + 128 * width].rearrange(
                "(p w) -> p w", w=width)
            v = op("gpsimd",
                   lambda e, dst=dst, src=src: e.dma_start(dst, src),
                   waits=[slot_free[j]], inc="sw", amt=16)
            return v, j

        xnsc = contextlib.ExitStack()
        xn = xnsc.enter_context(nc.sbuf_tensor("xn", [128, NDC * T], BF16))

        ln_scope = contextlib.ExitStack()
        lent = ln_scope.enter_context
        x_stage = lent(nc.sbuf_tensor("x_stage", [128, 2 * T], F32))
        xbft = lent(nc.sbuf_tensor("xbft", [128, T], BF16))
        sqbf = lent(nc.sbuf_tensor("sqbf", [128, T], BF16))
        meanbc = lent(nc.sbuf_tensor("meanbc", [128, T], F32))
        rstdbc = lent(nc.sbuf_tensor("rstdbc", [128, T], F32))
        r1 = lent(nc.sbuf_tensor("r1", [1, T], F32))
        r2 = lent(nc.sbuf_tensor("r2", [1, T], F32))
        r3 = lent(nc.sbuf_tensor("r3", [1, T], F32))
        tmp1 = lent(nc.sbuf_tensor("tmp1", [128, T], F32))
        ybuf2 = lent(nc.sbuf_tensor("ybuf2", [128, 2 * T], BF16))
        idxi = lent(nc.sbuf_tensor("idxi", [128, 64], I32))
        tkw_sb = lent(nc.sbuf_tensor("tkw_sb", [128, 64], F32))
        tok_sb = lent(nc.sbuf_tensor("tok_sb", [128, 64], F32))
        srw_sb = lent(nc.sbuf_tensor("srw_sb", [128, 64], F32))
        padt = lent(nc.sbuf_tensor("padt", [128, 512], F32))
        idxf = lent(nc.sbuf_tensor("idxf", [128, 64], F32))
        mask_all = lent(nc.sbuf_tensor("mask_all", [128, 64 * NE], F32))
        mincl_all = lent(nc.sbuf_tensor("mincl_all", [128, 64 * NE], F32))
        lastc = lent(nc.sbuf_tensor("lastc", [128, NE], F32))
        ctot32 = lent(nc.sbuf_tensor("ctot32", [128, NE], F32))
        mpos_all = lent(nc.sbuf_tensor("mpos_all", [128, 64 * NE], F32))
        mtmp = lent(nc.sbuf_tensor("mtmp", [128, 64], F32))
        macc = lent(nc.sbuf_tensor("macc", [128, 64], F32))
        addr_i = lent(nc.sbuf_tensor("addr_i", [128, 64], I32))
        val = lent(nc.sbuf_tensor("val", [128, 256], F32))
        tstage = lent(nc.sbuf_tensor("tstage", [128, 2 * 2048], BF16))

        psS = contextlib.ExitStack()
        psA = psS.enter_context(nc.psum_tensor("psA", [128, 2048], F32))
        psB = psS.enter_context(nc.psum_tensor("psB", [128, 2048], F32))

        op("gpsimd", lambda e: e.memset(ones128[:, :], 1.0))
        op("gpsimd", lambda e: e.memset(ones1f[:, :], 1.0))
        op("gpsimd", lambda e: e.memset(zcol[:, :], 0.0))
        op("gpsimd", lambda e: e.memset(epsc[:, :], 1e-5))
        op("gpsimd", lambda e: e.memset(macc[:, :], 0.0))
        m_pad = op("gpsimd", lambda e: e.memset(padt[:, :], PAD),
                   inc="sg")
        m_pad2 = op(
            "vector",
            lambda e: e.tensor_scalar_mul(
                padt[:, :].rearrange("p (a v) -> p a v", v=4)[:, :, 0:1],
                padt[:, :].rearrange("p (a v) -> p a v", v=4)[:, :, 0:1],
                0.0),
            waits=[("sg", m_pad)], inc="sv")
        gl_init = op(
            "gpsimd",
            lambda e: e.dma_start(
                glist.rearrange("(p a) v -> p (a v)", p=128),
                padt[:, 0:NE * CAP * 4 // 128]),
            waits=[("sg", m_pad), ("sv", m_pad2)], inc="sw", amt=16)
        idt_l = op("gpsimd", lambda e: e.dma_start(idt_sb[:, :], idt),
                   inc="sw", amt=16)
        for dst, src in ((bup_sb, bup), (bvp_sb, bvp), (bg2_sb, bg2p),
                         (bg1_sb, bg1p), (cwp_sb, cwp), (cbp_sb, cbp),
                         (ltri_sb, ltri), (tkw_sb, tkw), (tok_sb, tokid),
                         (srw_sb, srowid), (idxi, tki)):
            op("sync", lambda e, d=dst, s=src: e.dma_start(d[:, :], s),
               inc="sx", amt=16)
        sx_small = cnt["sx"]

        pre_w = [wflat_dma(0, 0, 2048), wflat_dma(0, GU, 2048),
                 wflat_dma(0, 2 * GU, 2048), wflat_dma(0, 3 * GU, 2048)]

        g_idx = op("gpsimd", lambda e: e.tensor_copy(idxf[:, :], idxi[:, :]),
                   waits=[("sx", sx_small)], inc="sg")
        scan_done = None
        for e_i in range(NE):
            m_e = op("gpsimd",
                     lambda e, e_i=e_i: e.tensor_single_scalar(
                         mask_all[:, e_i * 64:(e_i + 1) * 64], idxf[:, :],
                         float(e_i), ALU.is_equal),
                     waits=[("sg", g_idx)], inc="sg")
            scan_done = op(
                "vector",
                lambda e, e_i=e_i: e.tensor_tensor_scan(
                    mincl_all[:, e_i * 64:(e_i + 1) * 64],
                    mask_all[:, e_i * 64:(e_i + 1) * 64],
                    zcol[:, :].broadcast_to((128, 64)), 0.0,
                    ALU.add, ALU.add),
                waits=[("sg", m_e)], inc="sv")
        v_lastc = op(
            "vector",
            lambda e: e.tensor_copy(
                lastc[:, :].rearrange("p (e o) -> p e o", o=1),
                mincl_all[:, :].rearrange(
                    "p (e j) -> p e j", j=64)[:, :, 63:64]),
            waits=[("sv", scan_done)], inc="sv")
        subs_done = None
        for e_i in range(NE):
            sl64 = slice(e_i * 64, (e_i + 1) * 64)
            subs_done = op(
                "gpsimd",
                lambda e, sl64=sl64: e.tensor_sub(
                    mpos_all[:, sl64], mincl_all[:, sl64],
                    mask_all[:, sl64]),
                waits=[("sv", scan_done)],
                inc="sg" if e_i == NE - 1 else None)

        def xdma_op(dc, sl, w):
            return op(
                "sync",
                lambda e, dc=dc, sl=sl: e.dma_start(
                    x_stage[:, sl:sl + T], xt[dc * 128:(dc + 1) * 128, :]),
                waits=w, inc="sx", amt=16)

        xdma = {0: xdma_op(0, 0, []), 1: xdma_op(1, T, [])}
        cvt_done = {}
        mm_done = {}
        for dc in range(NDC):
            sl = (dc % 2) * T
            w = [("sx", xdma[dc])]
            if dc >= 1:
                w.append(("st", mm_done[dc - 1]))
            op("scalar",
               lambda e, sl=sl: e.copy(xbft[:, :], x_stage[:, sl:sl + T]),
               waits=w)
            cvt_done[dc] = op(
                "scalar",
                lambda e, sl=sl: e.activation(
                    sqbf[:, :], x_stage[:, sl:sl + T], AF.Square),
                inc="ss")
            for tb in range(NTB):
                op("tensor",
                   lambda e, dc=dc, tb=tb: e.matmul(
                       psA[0:1, tb * TB:(tb + 1) * TB], ones128[:, :],
                       xbft[:, tb * TB:(tb + 1) * TB],
                       start=(dc == 0), stop=(dc == NDC - 1)),
                   waits=[("ss", cvt_done[dc])])
            for tb in range(NTB):
                mm_done[dc] = op(
                    "tensor",
                    lambda e, dc=dc, tb=tb: e.matmul(
                        psB[0:1, tb * TB:(tb + 1) * TB], ones128[:, :],
                        sqbf[:, tb * TB:(tb + 1) * TB],
                        start=(dc == 0), stop=(dc == NDC - 1)),
                    inc="st" if tb == NTB - 1 else None)
            if dc + 2 < NDC:
                xdma[dc + 2] = xdma_op(dc + 2, (dc % 2) * T,
                                       [("ss", cvt_done[dc])])

        stats_done = mm_done[NDC - 1]
        inv = 1.0 / D
        v_mean = op("vector",
                    lambda e: e.tensor_scalar_mul(r1[:, :], psA[0:1, :], inv),
                    waits=[("st", stats_done)], inc="sv")
        op("vector", lambda e: e.tensor_mul(r2[:, :], r1[:, :], r1[:, :]))
        v_var = op("vector",
                   lambda e: e.scalar_tensor_tensor(
                       r2[:, :], psB[0:1, :], inv, r2[:, :],
                       ALU.mult, ALU.subtract),
                   inc="sv")
        s_sq = op("scalar",
                  lambda e: e.activation(r3[:, :], r2[:, :], AF.Sqrt,
                                         bias=epsc[0:1, :]),
                  waits=[("sv", v_var)], inc="ss")
        v_rec = op("vector", lambda e: e.reciprocal(r2[:, :], r3[:, :]),
                   waits=[("ss", s_sq)], inc="sv")

        rt_mm = op("tensor",
                   lambda e: e.matmul(psA[:, 1984:2016], ltri_sb[:, :],
                                      lastc[:, :], start=True, stop=True),
                   waits=[("st", stats_done), ("sv", v_lastc),
                          ("sv", v_mean)], inc="st")
        s_ctot = op("scalar",
                    lambda e: e.copy(ctot32[:, :], psA[:, 1984:2016]),
                    waits=[("st", rt_mm)], inc="ss")

        bc_done = None
        for tb in range(NTB):
            op("tensor",
               lambda e, tb=tb: e.matmul(
                   psA[:, tb * TB:(tb + 1) * TB], ones1f[:, :],
                   r1[0:1, tb * TB:(tb + 1) * TB], start=True, stop=True),
               waits=[("sv", v_rec), ("ss", s_ctot)])
        for tb in range(NTB):
            bc_done = op(
                "tensor",
                lambda e, tb=tb: e.matmul(
                    psB[:, tb * TB:(tb + 1) * TB], ones1f[:, :],
                    r2[0:1, tb * TB:(tb + 1) * TB], start=True, stop=True),
                inc="st" if tb == NTB - 1 else None)
        s_mean = op("scalar", lambda e: e.copy(meanbc[:, :], psA[:, :]),
                    waits=[("st", bc_done)], inc="ss")
        v_rstd = op("vector", lambda e: e.tensor_copy(rstdbc[:, :], psB[:, :]),
                    waits=[("st", bc_done)], inc="sv")

        vadd = {}
        for e_i in range(NE):
            sl64 = slice(e_i * 64, (e_i + 1) * 64)
            vadd[e_i] = op(
                "vector",
                lambda e, e_i=e_i, sl64=sl64: e.tensor_scalar(
                    mpos_all[:, sl64], mpos_all[:, sl64],
                    ctot32[:, e_i:e_i + 1],
                    float(e_i * CAP), ALU.add, ALU.add),
                waits=[("ss", s_ctot), ("sg", subs_done)], inc="sv")
        g_fin = None
        for e_i in range(NE):
            sl64 = slice(e_i * 64, (e_i + 1) * 64)
            op("gpsimd",
               lambda e, sl64=sl64: e.tensor_mul(mtmp[:, :],
                                                 mask_all[:, sl64],
                                                 mpos_all[:, sl64]),
               waits=[("sv", vadd[e_i])])
            g_fin = op("gpsimd",
                       lambda e: e.tensor_add(macc[:, :], macc[:, :],
                                              mtmp[:, :]),
                       inc="sg")
        v_addr = op("vector", lambda e: e.tensor_copy(addr_i[:, :],
                                                      macc[:, :]),
                    waits=[("sg", g_fin)], inc="sv")
        vz = op("vector", lambda e: e.memset(val[:, :], 0.0), inc="sv")
        vlast = vz
        for c, srcv in ((0, tok_sb), (1, srw_sb), (2, tkw_sb)):
            vlast = op(
                "vector",
                lambda e, c=c, srcv=srcv: e.tensor_copy(
                    val[:, :].rearrange(
                        "p (j c) -> p j c", c=4)[:, :, c:c + 1],
                    srcv[:, :].rearrange("p (j o) -> p j o", o=1)),
                waits=[("sv", vz), ("sx", sx_small)], inc="sv")
        glist_ready = 0
        for j in range(64):
            glist_ready = op(
                "gpsimd",
                lambda e, j=j: e.indirect_dma_start(
                    glist,
                    bass.IndirectOffsetOnAxis(ap=addr_i[:, j:j + 1], axis=0),
                    val[:, 4 * j:4 * j + 4], None),
                waits=[("sv", v_addr), ("sv", vlast), ("sw", gl_init)],
                inc="sw", amt=16)

        w0 = [("ss", s_mean), ("sv", v_rstd)]
        xdma2 = {0: xdma_op(0, 0, w0), 1: xdma_op(1, T, w0)}
        xn_dc = {}
        edge_done = {}
        cv_st = {}
        for dc in range(NDC):
            sl = (dc % 2) * T
            ysl = (dc % 2) * T
            w = [("sx", xdma2[dc]), ("ss", s_mean), ("sv", v_rstd)]
            if dc >= 1:
                w.append(("ss", edge_done[dc - 1]))
            op("vector",
               lambda e, sl=sl: e.tensor_sub(
                   tmp1[:, :], x_stage[:, sl:sl + T], meanbc[:, :]),
               waits=w)
            xn_dc[dc] = op(
                "vector",
                lambda e, dc=dc: e.tensor_mul(
                    xn[:, dc * T:(dc + 1) * T], tmp1[:, :], rstdbc[:, :]),
                inc="sv")
            wc1 = [("sv", xn_dc[dc]), ("sx", sx_small)]
            if dc >= 2:
                wc1.append(("sx", cv_st[dc - 2]))
            s_c1 = op("scalar",
                      lambda e, dc=dc: e.activation(
                          tmp1[:, :], xn[:, dc * T:(dc + 1) * T],
                          AF.Identity,
                          bias=cbp_sb[:, dc:dc + 1],
                          scale=cwp_sb[:, dc * 3 + 1:dc * 3 + 2]),
                      waits=wc1, inc="ss")
            op("vector",
               lambda e, dc=dc: e.scalar_tensor_tensor(
                   tmp1[:, 1:T], xn[:, dc * T:dc * T + T - 1],
                   cwp_sb[:, dc * 3:dc * 3 + 1], tmp1[:, 1:T],
                   ALU.mult, ALU.add),
               waits=[("ss", s_c1)])
            bdone = op(
                "vector",
                lambda e, dc=dc, ysl=ysl: e.scalar_tensor_tensor(
                    ybuf2[:, ysl:ysl + T - 1],
                    xn[:, dc * T + 1:(dc + 1) * T],
                    cwp_sb[:, dc * 3 + 2:dc * 3 + 3], tmp1[:, 0:T - 1],
                    ALU.mult, ALU.add),
                inc="sv")
            edge_done[dc] = op(
                "scalar",
                lambda e, ysl=ysl: e.copy(
                    ybuf2[:, ysl + T - 1:ysl + T], tmp1[:, T - 1:T]),
                waits=[("sv", bdone)], inc="ss")
            cv_st[dc] = op(
                "sync",
                lambda e, dc=dc, ysl=ysl: e.dma_start(
                    conv_h[:, dc * T:(dc + 1) * T], ybuf2[:, ysl:ysl + T]),
                waits=[("ss", edge_done[dc])], inc="sx", amt=16)
            if dc + 2 < NDC:
                xdma2[dc + 2] = xdma_op(dc + 2, (dc % 2) * T,
                                        [("sv", bdone)])

        psS.close()
        p2ps = contextlib.ExitStack()
        mmA = p2ps.enter_context(nc.psum_tensor("mmA", [128, 2048], F32))
        tin = p2ps.enter_context(nc.psum_tensor("tin", [128, 2048], BF16))
        otB = p2ps.enter_context(nc.psum_tensor("otB", [128, 2048], BF16))

        ps_guard = [("ss", s_mean), ("sv", v_rstd), ("ss", s_ctot)]
        tin_ev = {}
        xtm_st = {}
        gi = 0
        for tt in range(NDC):
            for g in range(4):
                ch = gi % 2
                wv = list(ps_guard) + [("sw", idt_l)]
                for q in range(4):
                    dc = g * 4 + q
                    wv.append(("sv", xn_dc[dc]))
                if gi >= 2:
                    wv.append(("ss", tin_ev[gi - 2]))
                tl = None
                for q in range(4):
                    dc = g * 4 + q
                    tl = op("tensor",
                            lambda e, dc=dc, tt=tt, ch=ch, q=q: e.transpose(
                                tin[:, ch * 1024 + q * 128:
                                    ch * 1024 + (q + 1) * 128],
                                xn[:, dc * T + tt * 128:dc * T + (tt + 1) * 128],
                                idt_sb[:, :]),
                            waits=wv, inc="st" if q == 3 else None)
                wev = [("st", tl)]
                if tt >= 2 and g == 0:
                    wev.append(("sx", xtm_st[tt - 2]))
                tin_ev[gi] = op(
                    "scalar",
                    lambda e, tt=tt, g=g, ch=ch: e.copy(
                        tstage[:, (tt % 2) * 2048 + g * 512:
                               (tt % 2) * 2048 + (g + 1) * 512],
                        tin[:, ch * 1024:ch * 1024 + 512]),
                    waits=wev, inc="ss")
                gi += 1
            xtm_st[tt] = op(
                "sync",
                lambda e, tt=tt: e.dma_start(
                    xn_tm[tt * 128:(tt + 1) * 128, :],
                    tstage[:, (tt % 2) * 2048:(tt % 2 + 1) * 2048]),
                waits=[("ss", tin_ev[gi - 1])], inc="sx", amt=16)
        xn_tm_ready = xtm_st[NDC - 1]
        xtm_tin_last = tin_ev[gi - 1]

        flush_block("p1")
        ln_scope.close()

        barrier_all()
        with (
            nc.sbuf_tensor("glt", [128, 2 * 4 * NCT], F32) as glt,
            nc.sbuf_tensor("gidx", [128, 2 * NCT], I32) as gidx,
            nc.sbuf_tensor("srow", [128, 4 * NCT], I32) as srow,
            nc.sbuf_tensor("xgst", [128, 2 * NCT * 2048], BF16) as xgst,
            nc.sbuf_tensor("xg_fm", [128, 2 * 16 * CAP], BF16) as xg_fm,
            nc.sbuf_tensor("hmid", [128, NIC * CAP], BF16) as hmid,
            nc.sbuf_tensor("silu_t", [128, 2 * CAP], BF16) as silu_t,
            nc.sbuf_tensor("o_fm", [128, 16 * CAP], BF16) as o_fm,
            nc.sbuf_tensor("o_sc", [128, 4 * 2048], BF16) as o_sc,
        ):
            z0 = op("gpsimd", lambda e: e.memset(xgst[:, :], 0.0),
                    inc="sg")

            glt_l = {}
            drv_done = {}
            gg_done = {}
            tinm_last = {}
            tinm_ev = {}
            gu_last = {}
            hmid_done = {}
            hmid_used = {}
            dn_ev = {}
            tout_last = {}
            osc_ev = {}
            sc_done = {}
            glt_used = {}

            def emit_gather(e_i):
                par = e_i % 2
                sp4 = e_i % 4
                w = [("sw", glist_ready)]
                if e_i - 2 in glt_used:
                    w.append(("sv", glt_used[e_i - 2]))
                glt_l[e_i] = op(
                    "sync",
                    lambda e, e_i=e_i, par=par: e.dma_start(
                        glt[:, par * 4 * NCT:(par + 1) * 4 * NCT].rearrange(
                            "p (ct v) -> p ct v", v=4),
                        glist[e_i * CAP:(e_i + 1) * CAP, :].rearrange(
                            "(ct p) v -> p ct v", p=128)),
                    waits=w, inc="sx", amt=16)
                g3 = glt[:, par * 4 * NCT:(par + 1) * 4 * NCT].rearrange(
                    "p (ct v) -> p ct v", v=4)
                wv = [("sx", glt_l[e_i])]
                if (e_i - 2, NCT - 1) in gg_done:
                    wv.append(("sw", gg_done[(e_i - 2, NCT - 1)]))
                if (e_i - 4, NCT - 1) in sc_done:
                    wv.append(("sw", sc_done[(e_i - 4, NCT - 1)]))
                op("vector",
                   lambda e, g3=g3, par=par: e.tensor_copy(
                       gidx[:, par * NCT:(par + 1) * NCT].rearrange(
                           "p (ct o) -> p ct o", o=1),
                       g3[:, :, 0:1]),
                   waits=wv)
                drv_done[e_i] = op(
                    "vector",
                    lambda e, g3=g3, sp4=sp4: e.tensor_copy(
                        srow[:, sp4 * NCT:(sp4 + 1) * NCT].rearrange(
                            "p (ct o) -> p ct o", o=1),
                        g3[:, :, 1:2]),
                    inc="sv")
                for ct in range(NCT):
                    w = [("sv", drv_done[e_i]), ("sg", z0),
                         ("sx", xn_tm_ready)]
                    if e_i - 2 in tinm_last:
                        w.append(("st", tinm_last[e_i - 2]))
                    gg_done[(e_i, ct)] = op(
                        "gpsimd",
                        lambda e, par=par, ct=ct:
                        e.indirect_dma_start(
                            xgst[:, (par * NCT + ct) * 2048:
                                 (par * NCT + ct + 1) * 2048], None,
                            xn_tm,
                            bass.IndirectOffsetOnAxis(
                                ap=gidx[:, par * NCT + ct:
                                        par * NCT + ct + 1], axis=0)),
                        waits=w, inc="sw", amt=16)

            def emit_tin(e_i):
                par = e_i % 2
                for hc in range(NDC):
                    ch = hc % 2
                    wv = [("sw", gg_done[(e_i, NCT - 1)]), ("sw", idt_l)]
                    if hc >= 2:
                        wv.append(("ss", tinm_ev[(e_i, hc - 2)]))
                    elif e_i >= 1:
                        wv.append(("ss", tinm_ev[(e_i - 1, NDC - 2 + hc)]))
                    else:
                        wv.append(("ss", xtm_tin_last))
                    tl = None
                    for ct in range(NCT):
                        tl = op(
                            "tensor",
                            lambda e, par=par, ct=ct, hc=hc, ch=ch:
                            e.transpose(
                                tin[:, ch * 1024 + ct * 128:
                                    ch * 1024 + (ct + 1) * 128],
                                xgst[:, (par * NCT + ct) * 2048 + hc * 128:
                                     (par * NCT + ct) * 2048 + (hc + 1) * 128],
                                idt_sb[:, :]),
                            waits=wv, inc="st" if ct == NCT - 1 else None)
                    if hc == NDC - 1:
                        tinm_last[e_i] = tl
                    wev = [("st", tl)]
                    if e_i >= 2:
                        wev.append(("st", gu_last[e_i - 2]))
                    tinm_ev[(e_i, hc)] = op(
                        "scalar",
                        lambda e, par=par, hc=hc, ch=ch: e.copy(
                            xg_fm[:, par * 16 * CAP + hc * CAP:
                                  par * 16 * CAP + (hc + 1) * CAP],
                            tin[:, ch * 1024:ch * 1024 + NCT * 128]),
                        waits=wev, inc="ss")

            emit_gather(0)
            emit_gather(1)

            bank = [(mmA, 0), (mmA, 1), (mmA, 2), (mmA, 3)]

            def bank_ap(i):
                t, q = bank[i]
                return t[:, q * TB:q * TB + CAP]

            bank_free = [("sv", v_rstd)] * 4
            bslot = [0]

            def next_bank(extra_waits=()):
                i = bslot[0] % 4
                bslot[0] += 1
                return i, [bank_free[i]] + list(extra_waits)

            def emit_scatters(e_i):
                par = e_i % 2
                sp4 = e_i % 4
                for ct in range(NCT):
                    sc_done[(e_i, ct)] = op(
                        "gpsimd",
                        lambda e, sp4=sp4, ct=ct:
                        e.indirect_dma_start(
                            y4,
                            bass.IndirectOffsetOnAxis(
                                ap=srow[:, sp4 * NCT + ct:
                                        sp4 * NCT + ct + 1], axis=0),
                            o_sc[:, ct * 2048:(ct + 1) * 2048],
                            None),
                        waits=[("sv", osc_ev[(e_i, ct, 0)]),
                               ("sv", osc_ev[(e_i, ct, 1)])],
                        inc="sw", amt=16)

            w_gu = {0: pre_w}

            for e_i in range(NE):
                par = e_i % 2
                if e_i >= 1:
                    emit_scatters(e_i - 1)
                emit_tin(e_i)
                g_of = par * 16 * CAP
                tin_all = ("ss", tinm_ev[(e_i, NDC - 1)])
                silu_done = {}
                for ic in range(NIC):
                    if ic < 2:
                        wg, jg = w_gu[e_i][2 * ic]
                        wu_, ju = w_gu[e_i][2 * ic + 1]
                    else:
                        wg, jg = wflat_dma(e_i, ic * 2 * GU, 2048)
                        wu_, ju = wflat_dma(e_i, ic * 2 * GU + GU, 2048)
                    bi, bw = next_bank([("sw", wg), tin_all])
                    g_last = None
                    for hc in range(NDC):
                        g_last = op(
                            "tensor",
                            lambda e, jg=jg, hc=hc, bi=bi, g_of=g_of:
                            e.matmul(
                                bank_ap(bi),
                                wring[:, jg * 2048 + hc * 128:
                                      jg * 2048 + (hc + 1) * 128],
                                xg_fm[:, g_of + hc * CAP:
                                      g_of + (hc + 1) * CAP],
                                start=(hc == 0), stop=(hc == NDC - 1)),
                            waits=bw,
                            inc="st" if hc == NDC - 1 else None)
                    slot_free[jg] = ("st", g_last)
                    sp = (ic % 2) * CAP
                    hprev = hmid_done.get(
                        (e_i, ic - 2),
                        hmid_done.get((e_i - 1, NIC - 2 + ic), 0))
                    silu_done[ic] = op(
                        "scalar",
                        lambda e, bi=bi, sp=sp: e.activation(
                            silu_t[:, sp:sp + CAP], bank_ap(bi), AF.Silu),
                        waits=[("st", g_last), ("sv", hprev)],
                        inc="ss")
                    bank_free[bi] = ("ss", silu_done[ic])
                    bi2, bw2 = next_bank([("sw", wu_)])
                    u_last = None
                    for hc in range(NDC):
                        u_last = op(
                            "tensor",
                            lambda e, ju=ju, hc=hc, bi2=bi2, g_of=g_of:
                            e.matmul(
                                bank_ap(bi2),
                                wring[:, ju * 2048 + hc * 128:
                                      ju * 2048 + (hc + 1) * 128],
                                xg_fm[:, g_of + hc * CAP:
                                      g_of + (hc + 1) * CAP],
                                start=(hc == 0), stop=(hc == NDC - 1)),
                            waits=bw2,
                            inc="st" if hc == NDC - 1 else None)
                    slot_free[ju] = ("st", u_last)
                    if ic == NIC - 1:
                        gu_last[e_i] = u_last
                    hmid_done[(e_i, ic)] = op(
                        "vector",
                        lambda e, bi2=bi2, ic=ic, sp=sp: e.tensor_mul(
                            hmid[:, ic * CAP:(ic + 1) * CAP],
                            bank_ap(bi2), silu_t[:, sp:sp + CAP]),
                        waits=[("st", u_last), ("ss", silu_done[ic]),
                               ("st", hmid_used.get(e_i - 1, 0))],
                        inc="sv")
                    bank_free[bi2] = ("sv", hmid_done[(e_i, ic)])
                for hc in range(NDC):
                    wd, jd = wflat_dma(e_i, 16 * GU + hc * DD, 1024)
                    bi, bw = next_bank([("sw", wd)])
                    o_last = None
                    for ic in range(NIC):
                        o_last = op(
                            "tensor",
                            lambda e, jd=jd, ic=ic, bi=bi: e.matmul(
                                bank_ap(bi),
                                wring[:, jd * 2048 + ic * 128:
                                      jd * 2048 + (ic + 1) * 128],
                                hmid[:, ic * CAP:(ic + 1) * CAP],
                                start=(ic == 0), stop=(ic == NIC - 1)),
                            waits=bw + [("sv", hmid_done[(e_i, ic)])],
                            inc="st" if ic == NIC - 1 else None)
                    slot_free[jd] = ("st", o_last)
                    if hc == NDC - 1:
                        hmid_used[e_i] = o_last
                    wev = [("st", o_last)]
                    if e_i >= 1:
                        wev.append(("st", tout_last[e_i - 1]))
                    dn_ev[(e_i, hc)] = op(
                        "scalar",
                        lambda e, bi=bi, hc=hc: e.copy(
                            o_fm[:, hc * CAP:(hc + 1) * CAP], bank_ap(bi)),
                        waits=wev, inc="ss")
                    bank_free[bi] = ("ss", dn_ev[(e_i, hc)])
                dn_all = ("ss", dn_ev[(e_i, NDC - 1)])
                for ct in range(NCT):
                    for half in range(2):
                        wv = [dn_all, ("sw", idt_l)]
                        prev = (e_i, ct - 1, half)
                        if ct == 0:
                            prev = (e_i - 1, NCT - 1, half)
                        if prev in osc_ev:
                            wv.append(("sv", osc_ev[prev]))
                        if e_i == 0:
                            wv.append(("sv", v_rstd))
                        tl = None
                        for q in range(8):
                            hc = half * 8 + q
                            tl = op(
                                "tensor",
                                lambda e, hc=hc, ct=ct, half=half, q=q:
                                e.transpose(
                                    otB[:, half * 1024 + q * 128:
                                        half * 1024 + (q + 1) * 128],
                                    o_fm[:, hc * CAP + ct * 128:
                                         hc * CAP + (ct + 1) * 128],
                                    idt_sb[:, :]),
                                waits=wv, inc="st" if q == 7 else None)
                        if ct == NCT - 1 and half == 1:
                            tout_last[e_i] = tl
                        wc = par * 4 * NCT + ct * 4 + 2
                        wev = [("st", tl)]
                        if (e_i - 1, ct) in sc_done:
                            wev.append(("sw", sc_done[(e_i - 1, ct)]))
                        osc_ev[(e_i, ct, half)] = op(
                            "vector",
                            lambda e, ct=ct, half=half, wc=wc:
                            e.tensor_scalar_mul(
                                o_sc[:, ct * 2048 + half * 1024:
                                     ct * 2048 + (half + 1) * 1024],
                                otB[:, half * 1024:(half + 1) * 1024],
                                glt[:, wc:wc + 1]),
                            waits=wev, inc="sv")
                glt_used[e_i] = osc_ev[(e_i, NCT - 1, 1)]
                if e_i + 1 < NE:
                    w_gu[e_i + 1] = [
                        wflat_dma(e_i + 1, 0, 2048),
                        wflat_dma(e_i + 1, GU, 2048),
                        wflat_dma(e_i + 1, 2 * GU, 2048),
                        wflat_dma(e_i + 1, 3 * GU, 2048)]
                if e_i + 2 < NE:
                    emit_gather(e_i + 2)
            emit_scatters(NE - 1)
            scatters_done = cnt["sw"]
            flush_block("moe")
        p2ps.close()

        barrier_all()
        with (
            nc.psum_tensor("psA2", [128, 2048], F32) as psA2,
            nc.psum_tensor("psB2", [128, 2048], F32) as psB2,
        ):
            at_scope = contextlib.ExitStack()
            wg1_bf = at_scope.enter_context(
                nc.sbuf_tensor("wg1_bf", [128, 8192], BF16))
            h_all = at_scope.enter_context(
                nc.sbuf_tensor("h_all", [128, NDHC * T], BF16))
            gates = at_scope.enter_context(
                nc.sbuf_tensor("gates", [128, T], F32))
            ug = at_scope.enter_context(nc.sbuf_tensor("ug", [128, T], F32))
            vg = at_scope.enter_context(nc.sbuf_tensor("vg", [128, T], F32))
            gost = at_scope.enter_context(
                nc.sbuf_tensor("gost", [128, 2 * T], BF16))
            d_wg1 = op("gpsimd", lambda e: e.dma_start(wg1_bf[:, :], wg1d),
                       inc="sw", amt=16)
            h_done = {}
            relu_done = {}
            ps_of = {0: psA2, 1: psB2}
            for dhc in range(NDHC):
                ps = ps_of[dhc % 2]
                for dc in range(NDC):
                    for tb in range(NTB):
                        h_done[dhc] = op(
                            "tensor",
                            lambda e, ps=ps, dc=dc, dhc=dhc, tb=tb:
                            e.matmul(
                                ps[:, tb * TB:(tb + 1) * TB],
                                wg1_bf[:, dc * 512 + dhc * 128:
                                       dc * 512 + (dhc + 1) * 128],
                                xn[:, dc * T + tb * TB:
                                   dc * T + (tb + 1) * TB],
                                start=(dc == 0), stop=(dc == NDC - 1)),
                            waits=[("sw", d_wg1),
                                   ("ss", relu_done.get(dhc - 2, 0))],
                            inc="st" if (dc == NDC - 1 and tb == NTB - 1)
                            else None)
                relu_done[dhc] = op(
                    "scalar",
                    lambda e, ps=ps, dhc=dhc: e.activation(
                        h_all[:, dhc * T:(dhc + 1) * T], ps[:, :], AF.Relu,
                        bias=bg1_sb[:, dhc:dhc + 1]),
                    waits=[("st", h_done[dhc])], inc="ss")
            h_ready = relu_done[NDHC - 1]

            psA_free = ("ss", relu_done.get(NDHC - 2, 0))
            psB_free = ("ss", h_ready)
            go_done = {}
            go_st = {}
            for rc in range(NRC):
                v2, jg2 = wblock_dma(wg2d[rc])
                vu, ju_ = wblock_dma(wud[rc])
                vv, jv_ = wblock_dma(wvd[rc])

                gmm = None
                for dhc in range(NDHC):
                    for tb in range(NTB):
                        gmm = op(
                            "tensor",
                            lambda e, jg2=jg2, dhc=dhc, tb=tb: e.matmul(
                                psB2[:, tb * TB:(tb + 1) * TB],
                                wring[:, jg2 * 2048 + dhc * 128:
                                      jg2 * 2048 + (dhc + 1) * 128],
                                h_all[:, dhc * T + tb * TB:
                                      dhc * T + (tb + 1) * TB],
                                start=(dhc == 0), stop=(dhc == NDHC - 1)),
                            waits=[("sw", v2), ("ss", h_ready), psB_free],
                            inc="st" if (dhc == NDHC - 1 and tb == NTB - 1)
                            else None)
                slot_free[jg2] = ("st", gmm)
                s_gate = op(
                    "scalar",
                    lambda e, rc=rc: e.activation(
                        gates[:, :], psB2[:, :], AF.Sigmoid,
                        bias=bg2_sb[:, rc:rc + 1]),
                    waits=[("st", gmm)], inc="ss")
                psB_free = ("ss", s_gate)
                umm = None
                for dc in range(NDC):
                    for tb in range(NTB):
                        umm = op(
                            "tensor",
                            lambda e, ju_=ju_, dc=dc, tb=tb: e.matmul(
                                psA2[:, tb * TB:(tb + 1) * TB],
                                wring[:, ju_ * 2048 + dc * 128:
                                      ju_ * 2048 + (dc + 1) * 128],
                                xn[:, dc * T + tb * TB:
                                   dc * T + (tb + 1) * TB],
                                start=(dc == 0), stop=(dc == NDC - 1)),
                            waits=[("sw", vu), psA_free],
                            inc="st" if (dc == NDC - 1 and tb == NTB - 1)
                            else None)
                slot_free[ju_] = ("st", umm)
                v_ug = op("vector",
                          lambda e, rc=rc: e.scalar_tensor_tensor(
                              ug[:, :], psA2[:, :], bup_sb[:, rc:rc + 1],
                              gates[:, :], ALU.add, ALU.mult),
                          waits=[("st", umm), ("ss", s_gate)], inc="sv")
                psA_free = ("sv", v_ug)
                vmm = None
                for dc in range(NDC):
                    for tb in range(NTB):
                        vmm = op(
                            "tensor",
                            lambda e, jv_=jv_, dc=dc, tb=tb: e.matmul(
                                psB2[:, tb * TB:(tb + 1) * TB],
                                wring[:, jv_ * 2048 + dc * 128:
                                      jv_ * 2048 + (dc + 1) * 128],
                                xn[:, dc * T + tb * TB:
                                   dc * T + (tb + 1) * TB],
                                start=(dc == 0), stop=(dc == NDC - 1)),
                            waits=[("sw", vv), psB_free],
                            inc="st" if (dc == NDC - 1 and tb == NTB - 1)
                            else None)
                slot_free[jv_] = ("st", vmm)
                v_vg = op("vector",
                          lambda e, rc=rc: e.scalar_tensor_tensor(
                              vg[:, :], psB2[:, :], bvp_sb[:, rc:rc + 1],
                              gates[:, :], ALU.add, ALU.mult),
                          waits=[("st", vmm)], inc="sv")
                psB_free = ("sv", v_vg)
                op("vector",
                   lambda e: e.tensor_tensor_scan(
                       ug[:, :], ug[:, :],
                       zcol[:, :].broadcast_to((128, T)), 0.0,
                       ALU.add, ALU.add))
                wgo = []
                if rc >= 2:
                    wgo.append(("sx", go_st[rc - 2]))
                go_done[rc] = op(
                    "vector",
                    lambda e, rc=rc: e.tensor_mul(
                        gost[:, (rc % 2) * T:(rc % 2 + 1) * T],
                        ug[:, :], vg[:, :]),
                    waits=wgo, inc="sv")
                go_st[rc] = op(
                    "sync",
                    lambda e, rc=rc: e.dma_start(
                        go_h[rc * 128:(rc + 1) * 128, :],
                        gost[:, (rc % 2) * T:(rc % 2 + 1) * T]),
                    waits=[("sv", go_done[rc])], inc="sx", amt=16)
            go_ready = go_st[NRC - 1]
            flush_block("attn")
            at_scope.close()
        xnsc.close()

        barrier_all()
        with (
            nc.psum_tensor("psU", [128, 2048], F32) as psU,
            nc.psum_tensor("ptY", [128, 4096], BF16) as ptY,
            nc.sbuf_tensor("wud_all", [128, NDC * 2048], BF16) as wud_all,
            nc.sbuf_tensor("go_stage", [128, 2 * NRC * TB], BF16) as go_stage,
            nc.sbuf_tensor("ysum", [128, 2 * 4 * 2048], BF16) as ysum,
            nc.sbuf_tensor("ystage", [128, 4 * 2048], BF16) as ystage,
            nc.sbuf_tensor("x_st2", [128, 4 * TB], F32) as x_st2,
            nc.sbuf_tensor("cstr", [128, 4 * TB], BF16) as cstr,
            nc.sbuf_tensor("ostg", [128, 4 * TB], F32) as ostg,
        ):
            wud_l = {}
            for dc in range(NDC):
                wud_l[dc] = op(
                    "gpsimd",
                    lambda e, dc=dc: e.dma_start(
                        wud_all[:, dc * 2048:(dc + 1) * 2048], wud[dc]),
                    inc="sw", amt=16)

            yld = {}
            ysum_done = {}
            gol = {}
            f1d = {}
            f2d = {}
            out_st = {}

            def emit_prepass(tb):
                par = tb % 2
                gw = [("sx", go_ready)]
                if tb >= 2:
                    gw.append(("st", uv_l[(tb - 2, NDC - 1)]))
                gol[tb] = op(
                    "sync",
                    lambda e, tb=tb, par=par: e.dma_start(
                        go_stage[:, par * NRC * TB:(par + 1) * NRC * TB]
                        .rearrange("p (rc c) -> p rc c", rc=NRC),
                        go_h[:, tb * TB:(tb + 1) * TB].rearrange(
                            "(rc p) c -> p rc c", p=128)),
                    waits=gw, inc="sx", amt=16)
                for k in range(NK):
                    i = tb * NK + k
                    w = [("sw", scatters_done)]
                    if i >= 1:
                        w.append(("sv", ysum_done[((i - 1) // NK,
                                                   (i - 1) % NK)]))
                    yld[(tb, k)] = op(
                        "scalar",
                        lambda e, tb=tb, k=k: e.dma_start(
                            ystage[:, :].rearrange("p (s f) -> p s f", s=4),
                            y4[k * T + tb * 512:k * T + (tb + 1) * 512, :]
                            .rearrange("(s p) f -> p s f", p=128)),
                        waits=w, inc="sy", amt=16)
                    par2 = (tb % 2) * 8192
                    wv = [("sy", yld[(tb, k)])]
                    if k == 0 and tb >= 2:
                        wv.append(("sv", f2d[(tb - 2, NDC - 1)]))
                    if k == 0:
                        ysum_done[(tb, k)] = op(
                            "vector",
                            lambda e, par2=par2: e.tensor_copy(
                                ysum[:, par2:par2 + 8192], ystage[:, :]),
                            waits=wv, inc="sv")
                    else:
                        ysum_done[(tb, k)] = op(
                            "vector",
                            lambda e, par2=par2: e.tensor_add(
                                ysum[:, par2:par2 + 8192],
                                ysum[:, par2:par2 + 8192], ystage[:, :]),
                            waits=wv, inc="sv")

            uv_l = {}
            emit_prepass(0)
            emit_prepass(1)
            for tb in range(NTB):
                par = tb % 2
                g_of = par * NRC * TB
                y_of = par * 8192
                for dc in range(NDC):
                    i = tb * NDC + dc
                    sl = (i % 4) * TB
                    wld = []
                    if i >= 4:
                        wld.append(("sv", f1d[((i - 4) // NDC,
                                               (i - 4) % NDC)]))
                    op("sync",
                       lambda e, dc=dc, tb=tb, sl=sl: e.dma_start(
                           x_st2[:, sl:sl + TB],
                           xt[dc * 128:(dc + 1) * 128,
                              tb * TB:(tb + 1) * TB]),
                       waits=wld, inc="sx", amt=16)
                    cl = op("sync",
                            lambda e, dc=dc, tb=tb, sl=sl: e.dma_start(
                                cstr[:, sl:sl + TB],
                                conv_h[:, dc * T + tb * TB:
                                       dc * T + (tb + 1) * TB]),
                            inc="sx", amt=16)
                    uvw = [("sw", wud_l[dc]), ("sx", gol[tb])]
                    if i >= 4:
                        uvw.append(("sv", f1d[((i - 4) // NDC,
                                               (i - 4) % NDC)]))
                    uv = None
                    for rc in range(NRC):
                        uv = op(
                            "tensor",
                            lambda e, dc=dc, rc=rc, g_of=g_of, sl=sl:
                            e.matmul(
                                psU[:, sl:sl + TB],
                                wud_all[:, dc * 2048 + rc * 128:
                                        dc * 2048 + (rc + 1) * 128],
                                go_stage[:, g_of + rc * TB:
                                         g_of + (rc + 1) * TB],
                                start=(rc == 0), stop=(rc == NRC - 1)),
                            waits=uvw,
                            inc="st" if rc == NRC - 1 else None)
                    uv_l[(tb, dc)] = uv
                    tw = [("sv", ysum_done[(tb, NK - 1)]), ("sw", idt_l)]
                    if i >= 4:
                        tw.append(("sv", f2d[((i - 4) // NDC,
                                              (i - 4) % NDC)]))
                    sly = (i % 4) * 1024
                    ty = None
                    for s in range(4):
                        ty = op(
                            "tensor",
                            lambda e, s=s, dc=dc, y_of=y_of, sly=sly:
                            e.transpose(
                                ptY[:, sly + s * 128:sly + (s + 1) * 128],
                                ysum[:, y_of + s * 2048 + dc * 128:
                                     y_of + s * 2048 + (dc + 1) * 128],
                                idt_sb[:, :]),
                            waits=tw, inc="st" if s == 3 else None)
                    wf = [("st", uv), ("sx", cl)]
                    if i >= 4:
                        wf.append(("so", out_st[i - 4]))
                    op("vector",
                       lambda e, sl=sl: e.tensor_add(
                           ostg[:, sl:sl + TB], psU[:, sl:sl + TB],
                           x_st2[:, sl:sl + TB]),
                       waits=wf)
                    f1d[(tb, dc)] = op(
                        "vector",
                        lambda e, sl=sl: e.tensor_add(
                            ostg[:, sl:sl + TB], ostg[:, sl:sl + TB],
                            cstr[:, sl:sl + TB]),
                        inc="sv")
                    f2d[(tb, dc)] = op(
                        "vector",
                        lambda e, sl=sl, sly=sly: e.tensor_add(
                            ostg[:, sl:sl + TB], ostg[:, sl:sl + TB],
                            ptY[:, sly:sly + TB]),
                        waits=[("st", ty)], inc="sv")
                    out_st[i] = op(
                        "sync",
                        lambda e, dc=dc, tb=tb, sl=sl: e.dma_start(
                            outt[dc * 128:(dc + 1) * 128,
                                 tb * TB:(tb + 1) * TB],
                            ostg[:, sl:sl + TB]),
                        waits=[("sv", f2d[(tb, dc)])], inc="so", amt=16)
                if tb + 2 < NTB:
                    emit_prepass(tb + 2)
            wait_only("sync", [("so", cnt["so"])])
            flush_block("final")

    return nc


def _pack_t4(a):
    n = a.shape[0] // 128
    c = a.shape[1] // 128
    return np.ascontiguousarray(
        a.reshape(n, 128, c, 128).transpose(2, 1, 0, 3).reshape(
            c, 128, n * 128))


def kernel(L0_attn_x, L0_attn_Wu, L0_attn_bu, L0_attn_Wv, L0_attn_bv,
           L0_attn_Wg1, L0_attn_bg1, L0_attn_Wg2, L0_attn_bg2,
           L0_attn_conv_w, L0_attn_conv_b,
           L0_ffn_gate_weight, L0_ffn_up_weight, L0_ffn_down_weight,
           L0_ffn_topk_idx, L0_ffn_topk_weight):
    x = np.asarray(L0_attn_x, dtype=np.float32)
    Wu = np.asarray(L0_attn_Wu, dtype=np.float32)
    Wv = np.asarray(L0_attn_Wv, dtype=np.float32)
    Wg1 = np.asarray(L0_attn_Wg1, dtype=np.float32)
    Wg2 = np.asarray(L0_attn_Wg2, dtype=np.float32)
    bu = np.asarray(L0_attn_bu, dtype=np.float32)
    bv = np.asarray(L0_attn_bv, dtype=np.float32)
    bg1 = np.asarray(L0_attn_bg1, dtype=np.float32)
    bg2 = np.asarray(L0_attn_bg2, dtype=np.float32)
    convw = np.asarray(L0_attn_conv_w, dtype=np.float32)
    convb = np.asarray(L0_attn_conv_b, dtype=np.float32)
    gw = np.asarray(L0_ffn_gate_weight, dtype=np.float32)
    uw = np.asarray(L0_ffn_up_weight, dtype=np.float32)
    dw = np.asarray(L0_ffn_down_weight, dtype=np.float32)
    tidx = np.asarray(L0_ffn_topk_idx, dtype=np.int32)
    tw = np.asarray(L0_ffn_topk_weight, dtype=np.float32)

    B = x.shape[0]

    wud = _pack_t4(np.ascontiguousarray(Wu.T))
    wvd = _pack_t4(np.ascontiguousarray(Wv.T))
    wg2d = _pack_t4(np.ascontiguousarray(Wg2.T))
    wg1d = np.ascontiguousarray(
        Wg1.T.reshape(16, 128, 512).transpose(1, 0, 2).reshape(128, 8192))
    moe_blocks = []
    for e in range(NE):
        gt = _pack_t4(np.ascontiguousarray(gw[e].T))
        ut = _pack_t4(np.ascontiguousarray(uw[e].T))
        dt = _pack_t4(np.ascontiguousarray(dw[e].T))
        parts = []
        for ic in range(NIC):
            parts.append(gt[ic].ravel())
            parts.append(ut[ic].ravel())
        for hc in range(NDC):
            parts.append(dt[hc].ravel())
        moe_blocks.append(np.concatenate(parts))
    wmoe_np = np.ascontiguousarray(np.stack(moe_blocks))
    del moe_blocks

    bup_np = np.ascontiguousarray(bu.reshape(16, 128).T)
    bvp_np = np.ascontiguousarray(bv.reshape(16, 128).T)
    bg2_np = np.ascontiguousarray(bg2.reshape(16, 128).T)
    bg1_np = np.ascontiguousarray(bg1.reshape(4, 128).T)
    cwp_np = np.ascontiguousarray(
        convw[:, 0, :].reshape(16, 128, 3).transpose(1, 0, 2).reshape(128, 48))
    cbp_np = np.ascontiguousarray(convb.reshape(16, 128).T)

    s_all = np.arange(NSLOT)
    t_all = s_all // NK
    k_all = s_all % NK
    tok_np = np.ascontiguousarray(t_all.astype(np.float32).reshape(128, 64))
    srow_np = np.ascontiguousarray(
        (k_all * T + t_all).astype(np.float32).reshape(128, 64))
    ltri_np = np.ascontiguousarray(np.triu(np.ones((128, 128), np.float32), 1))
    idt_np = np.eye(128, dtype=np.float32)

    nc = _build_program()
    in_maps = []
    for b in range(B):
        ts = tidx[b * T:(b + 1) * T].reshape(128, 64)
        ws = tw[b * T:(b + 1) * T].reshape(128, 64)
        in_maps.append({
            "xt": np.ascontiguousarray(x[b].T),
            "wud": wud, "wvd": wvd, "wg2d": wg2d, "wg1d": wg1d,
            "wmoe": wmoe_np,
            "bup": bup_np, "bvp": bvp_np, "bg2p": bg2_np, "bg1p": bg1_np,
            "cwp": cwp_np, "cbp": cbp_np,
            "tki": np.ascontiguousarray(ts),
            "tkw": np.ascontiguousarray(ws),
            "tokid": tok_np, "srowid": srow_np, "ltri": ltri_np,
            "idt": idt_np,
        })
    res = run_bass_kernel_spmd(nc, in_maps, core_ids=list(range(B)))
    out = np.empty((B, T, D), dtype=np.float32)
    for b in range(B):
        out[b] = np.asarray(res.results[b]["outt"]).T
    return out


# version 13
# speedup vs baseline: 1.7235x; 1.0839x over previous
import contextlib

import numpy as np
import concourse.bass as bass
import concourse.mybir as mybir
from concourse.bass_utils import run_bass_kernel_spmd

F32 = mybir.dt.float32
BF16 = mybir.dt.bfloat16
I32 = mybir.dt.int32
AF = mybir.ActivationFunctionType
ALU = mybir.AluOpType

D = 2048
T = 2048
R = 2048
DH = 512
DI = 1024
NE = 32
NK = 4
CAP = 384
NCT = CAP // 128
NDC = D // 128
NRC = R // 128
NIC = DI // 128
NDHC = DH // 128
TB = 512
NTB = T // TB
GU = 128 * 2048
DD = 128 * 1024
EXP_F = 16 * GU + 16 * DD
NSLOT = T * NK
PAD = 8192.0

ENGINES = ("sync", "tensor", "vector", "scalar", "gpsimd")
SEMS = ("sw", "sx", "sy", "st", "sv", "ss", "sg", "so")


def _build_program():
    nc = bass.Bass("TRN2", target_bir_lowering=False)

    xt = nc.dram_tensor("xt", [D, T], F32, kind="ExternalInput").ap()
    wud = nc.dram_tensor("wud", [16, 128, 2048], F32, kind="ExternalInput").ap()
    wvd = nc.dram_tensor("wvd", [16, 128, 2048], F32, kind="ExternalInput").ap()
    wg2d = nc.dram_tensor("wg2d", [16, 128, 512], F32, kind="ExternalInput").ap()
    wg1d = nc.dram_tensor("wg1d", [128, 8192], F32, kind="ExternalInput").ap()
    wmoe = nc.dram_tensor("wmoe", [NE, EXP_F], F32, kind="ExternalInput").ap()
    bup = nc.dram_tensor("bup", [128, 16], F32, kind="ExternalInput").ap()
    bvp = nc.dram_tensor("bvp", [128, 16], F32, kind="ExternalInput").ap()
    bg2p = nc.dram_tensor("bg2p", [128, 16], F32, kind="ExternalInput").ap()
    bg1p = nc.dram_tensor("bg1p", [128, 4], F32, kind="ExternalInput").ap()
    cwp = nc.dram_tensor("cwp", [128, 48], F32, kind="ExternalInput").ap()
    cbp = nc.dram_tensor("cbp", [128, 16], F32, kind="ExternalInput").ap()
    tki = nc.dram_tensor("tki", [128, 64], I32, kind="ExternalInput").ap()
    tkw = nc.dram_tensor("tkw", [128, 64], F32, kind="ExternalInput").ap()
    tokid = nc.dram_tensor("tokid", [128, 64], F32, kind="ExternalInput").ap()
    srowid = nc.dram_tensor("srowid", [128, 64], F32, kind="ExternalInput").ap()
    ltri = nc.dram_tensor("ltri", [128, 128], F32, kind="ExternalInput").ap()
    idt = nc.dram_tensor("idt", [128, 128], F32, kind="ExternalInput").ap()
    cwd = nc.dram_tensor("cwd", [128, 16 * 384], F32, kind="ExternalInput").ap()

    outt = nc.dram_tensor("outt", [D, T], F32, kind="ExternalOutput").ap()
    conv_h = nc.dram_tensor("conv_h", [128, NDC * T], BF16, kind="Internal").ap()
    xn_tm = nc.dram_tensor("xn_tm", [T, D], BF16, kind="Internal").ap()
    y4 = nc.dram_tensor("y4", [NSLOT + 128, D], BF16, kind="Internal").ap()
    glist = nc.dram_tensor("glist", [NE * CAP, 4], F32, kind="Internal").ap()
    go_h = nc.dram_tensor("go_h", [NTB, 128, NRC, TB], BF16, kind="Internal").ap()

    lists = {e: [] for e in ENGINES}
    cnt = {s: 0 for s in SEMS}
    waited = {e: {} for e in ENGINES}
    semh = {}

    def op(eng, builder, waits=(), inc=None, amt=1):
        need = tuple((s, v) for (s, v) in waits
                     if v and v > waited[eng].get(s, 0))
        for s, v in need:
            waited[eng][s] = v
        val = None
        if inc is not None:
            cnt[inc] += amt
            val = cnt[inc]

        def f(e, _n=need, _b=builder, _i=inc, _a=amt):
            for s, v in _n:
                e.wait_ge(semh[s], v)
            if _b is not None:
                r = _b(e)
                if _i is not None:
                    r.then_inc(semh[_i], _a)
        lists[eng].append(f)
        return val

    def wait_only(eng, waits):
        op(eng, None, waits=waits)

    def barrier_all():
        b = tuple(cnt.items())
        for e in ENGINES:
            wait_only(e, b)

    def flush_block(name):
        L = {e: lists[e] for e in ENGINES}
        for e in ENGINES:
            lists[e] = []
        with nc.Block(name) as blk:
            @blk.sync
            def _(e):
                for f in L["sync"]:
                    f(e)

            @blk.tensor
            def _(e):
                for f in L["tensor"]:
                    f(e)

            @blk.vector
            def _(e):
                for f in L["vector"]:
                    f(e)

            @blk.scalar
            def _(e):
                for f in L["scalar"]:
                    f(e)

            @blk.gpsimd
            def _(e):
                for f in L["gpsimd"]:
                    f(e)

    with contextlib.ExitStack() as stack:
        ent = stack.enter_context

        for s in SEMS:
            semh[s] = ent(nc.semaphore(s))

        ones128 = ent(nc.sbuf_tensor("ones128", [128, 1], BF16))
        ones1f = ent(nc.sbuf_tensor("ones1f", [1, 128], F32))
        zcol = ent(nc.sbuf_tensor("zcol", [128, 1], F32))
        bup_sb = ent(nc.sbuf_tensor("bup_sb", [128, 16], F32))
        bvp_sb = ent(nc.sbuf_tensor("bvp_sb", [128, 16], F32))
        bg2_sb = ent(nc.sbuf_tensor("bg2_sb", [128, 16], F32))
        bg1_sb = ent(nc.sbuf_tensor("bg1_sb", [128, 4], F32))
        cwp_sb = ent(nc.sbuf_tensor("cwp_sb", [128, 48], F32))
        cbp_sb = ent(nc.sbuf_tensor("cbp_sb", [128, 16], F32))
        epsc = ent(nc.sbuf_tensor("epsc", [1, 1], F32))
        ltri_sb = ent(nc.sbuf_tensor("ltri_sb", [128, 128], F32))
        idt_sb = ent(nc.sbuf_tensor("idt_sb", [128, 128], BF16))
        wring = ent(nc.sbuf_tensor("wring", [128, 6 * 2048], BF16))

        slot_free = [("st", 0)] * 6
        blk_i = [0]

        def wblock_dma(src_ap, waits=()):
            j = blk_i[0] % 6
            blk_i[0] += 1
            dst = wring[:, j * 2048: j * 2048 + src_ap.shape[-1]]
            v = op("gpsimd",
                   lambda e, dst=dst, src=src_ap: e.dma_start(dst, src),
                   waits=[slot_free[j]] + list(waits), inc="sw", amt=16)
            return v, j

        def wflat_dma(row, off, width):
            j = blk_i[0] % 6
            blk_i[0] += 1
            dst = wring[:, j * 2048: j * 2048 + width]
            src = wmoe[row, off: off + 128 * width].rearrange(
                "(p w) -> p w", w=width)
            v = op("gpsimd",
                   lambda e, dst=dst, src=src: e.dma_start(dst, src),
                   waits=[slot_free[j]], inc="sw", amt=16)
            return v, j

        xnsc = contextlib.ExitStack()
        xn = xnsc.enter_context(nc.sbuf_tensor("xn", [128, NDC * T], BF16))

        ln_scope = contextlib.ExitStack()
        lent = ln_scope.enter_context
        x_stage = lent(nc.sbuf_tensor("x_stage", [128, 2 * T], F32))
        xbft = lent(nc.sbuf_tensor("xbft", [128, T], BF16))
        sqbf = lent(nc.sbuf_tensor("sqbf", [128, T], BF16))
        meanbc = lent(nc.sbuf_tensor("meanbc", [128, T], F32))
        rstdbc = lent(nc.sbuf_tensor("rstdbc", [128, T], F32))
        r1 = lent(nc.sbuf_tensor("r1", [1, T], F32))
        r2 = lent(nc.sbuf_tensor("r2", [1, T], F32))
        r3 = lent(nc.sbuf_tensor("r3", [1, T], F32))
        tmp1 = lent(nc.sbuf_tensor("tmp1", [128, T], F32))
        ybuf2 = lent(nc.sbuf_tensor("ybuf2", [128, 2 * T], BF16))
        idxi = lent(nc.sbuf_tensor("idxi", [128, 64], I32))
        tkw_sb = lent(nc.sbuf_tensor("tkw_sb", [128, 64], F32))
        tok_sb = lent(nc.sbuf_tensor("tok_sb", [128, 64], F32))
        srw_sb = lent(nc.sbuf_tensor("srw_sb", [128, 64], F32))
        padt = lent(nc.sbuf_tensor("padt", [128, 512], F32))
        idxf = lent(nc.sbuf_tensor("idxf", [128, 64], F32))
        mq = lent(nc.sbuf_tensor("mq", [128, 2 * 64], F32))
        m2 = lent(nc.sbuf_tensor("m2", [128, 64], F32))
        mincl_all = lent(nc.sbuf_tensor("mincl_all", [128, 64 * NE], F32))
        lastc = lent(nc.sbuf_tensor("lastc", [128, NE], F32))
        ctot32 = lent(nc.sbuf_tensor("ctot32", [128, NE], F32))
        mtmp = lent(nc.sbuf_tensor("mtmp", [128, 64], F32))
        macc = lent(nc.sbuf_tensor("macc", [128, 64], F32))
        addr_i = lent(nc.sbuf_tensor("addr_i", [128, 64], I32))
        val = lent(nc.sbuf_tensor("val", [128, 256], F32))
        tstage = lent(nc.sbuf_tensor("tstage", [128, 2 * 2048], BF16))
        cwd_sb = lent(nc.sbuf_tensor("cwd_sb", [128, 16 * 384], BF16))

        psS = contextlib.ExitStack()
        psA = psS.enter_context(nc.psum_tensor("psA", [128, 2048], F32))
        psB = psS.enter_context(nc.psum_tensor("psB", [128, 2048], F32))

        op("gpsimd", lambda e: e.memset(ones128[:, :], 1.0))
        op("gpsimd", lambda e: e.memset(ones1f[:, :], 1.0))
        op("gpsimd", lambda e: e.memset(zcol[:, :], 0.0))
        op("gpsimd", lambda e: e.memset(epsc[:, :], 1e-5))
        op("gpsimd", lambda e: e.memset(macc[:, :], 0.0))
        m_pad = op("gpsimd", lambda e: e.memset(padt[:, :], PAD),
                   inc="sg")
        m_pad2 = op(
            "vector",
            lambda e: e.tensor_scalar_mul(
                padt[:, :].rearrange("p (a v) -> p a v", v=4)[:, :, 0:1],
                padt[:, :].rearrange("p (a v) -> p a v", v=4)[:, :, 0:1],
                0.0),
            waits=[("sg", m_pad)], inc="sv")
        gl_init = op(
            "gpsimd",
            lambda e: e.dma_start(
                glist.rearrange("(p a) v -> p (a v)", p=128),
                padt[:, 0:NE * CAP * 4 // 128]),
            waits=[("sg", m_pad), ("sv", m_pad2)], inc="sw", amt=16)
        idt_l = op("gpsimd", lambda e: e.dma_start(idt_sb[:, :], idt),
                   inc="sw", amt=16)
        cwd_l = op("gpsimd", lambda e: e.dma_start(cwd_sb[:, :], cwd),
                   inc="sw", amt=16)
        for dst, src in ((bup_sb, bup), (bvp_sb, bvp), (bg2_sb, bg2p),
                         (bg1_sb, bg1p), (cwp_sb, cwp), (cbp_sb, cbp),
                         (ltri_sb, ltri), (tkw_sb, tkw), (tok_sb, tokid),
                         (srw_sb, srowid), (idxi, tki)):
            op("sync", lambda e, d=dst, s=src: e.dma_start(d[:, :], s),
               inc="sx", amt=16)
        sx_small = cnt["sx"]

        pre_w = [wflat_dma(0, 0, 2048), wflat_dma(0, GU, 2048),
                 wflat_dma(0, 2 * GU, 2048), wflat_dma(0, 3 * GU, 2048),
                 wflat_dma(0, 4 * GU, 2048), wflat_dma(0, 5 * GU, 2048)]

        g_idx = op("gpsimd", lambda e: e.tensor_copy(idxf[:, :], idxi[:, :]),
                   waits=[("sx", sx_small)], inc="sg")
        scan_done = None
        scans = {}
        for e_i in range(NE):
            par = e_i % 2
            wm = [("sg", g_idx)]
            if e_i >= 2:
                wm.append(("sv", scans[e_i - 2]))
            m_e = op("gpsimd",
                     lambda e, e_i=e_i, par=par: e.tensor_single_scalar(
                         mq[:, par * 64:(par + 1) * 64], idxf[:, :],
                         float(e_i), ALU.is_equal),
                     waits=wm, inc="sg")
            scans[e_i] = scan_done = op(
                "vector",
                lambda e, e_i=e_i, par=par: e.tensor_tensor_scan(
                    mincl_all[:, e_i * 64:(e_i + 1) * 64],
                    mq[:, par * 64:(par + 1) * 64],
                    zcol[:, :].broadcast_to((128, 64)), 0.0,
                    ALU.add, ALU.add),
                waits=[("sg", m_e)], inc="sv")
        v_lastc = op(
            "vector",
            lambda e: e.tensor_copy(
                lastc[:, :].rearrange("p (e o) -> p e o", o=1),
                mincl_all[:, :].rearrange(
                    "p (e j) -> p e j", j=64)[:, :, 63:64]),
            waits=[("sv", scan_done)], inc="sv")

        def xdma_op(dc, sl, w):
            return op(
                "sync",
                lambda e, dc=dc, sl=sl: e.dma_start(
                    x_stage[:, sl:sl + T], xt[dc * 128:(dc + 1) * 128, :]),
                waits=w, inc="sx", amt=16)

        xdma = {0: xdma_op(0, 0, []), 1: xdma_op(1, T, [])}
        cvt_done = {}
        mm_done = {}
        for dc in range(NDC):
            sl = (dc % 2) * T
            w = [("sx", xdma[dc])]
            if dc >= 1:
                w.append(("st", mm_done[dc - 1]))
            op("scalar",
               lambda e, sl=sl: e.copy(xbft[:, :], x_stage[:, sl:sl + T]),
               waits=w)
            cvt_done[dc] = op(
                "scalar",
                lambda e, sl=sl: e.activation(
                    sqbf[:, :], x_stage[:, sl:sl + T], AF.Square),
                inc="ss")
            for tb in range(NTB):
                op("tensor",
                   lambda e, dc=dc, tb=tb: e.matmul(
                       psA[0:1, tb * TB:(tb + 1) * TB], ones128[:, :],
                       xbft[:, tb * TB:(tb + 1) * TB],
                       start=(dc == 0), stop=(dc == NDC - 1)),
                   waits=[("ss", cvt_done[dc])])
            for tb in range(NTB):
                mm_done[dc] = op(
                    "tensor",
                    lambda e, dc=dc, tb=tb: e.matmul(
                        psB[0:1, tb * TB:(tb + 1) * TB], ones128[:, :],
                        sqbf[:, tb * TB:(tb + 1) * TB],
                        start=(dc == 0), stop=(dc == NDC - 1)),
                    inc="st" if tb == NTB - 1 else None)
            if dc + 2 < NDC:
                xdma[dc + 2] = xdma_op(dc + 2, (dc % 2) * T,
                                       [("ss", cvt_done[dc])])

        stats_done = mm_done[NDC - 1]
        inv = 1.0 / D
        v_mean = op("vector",
                    lambda e: e.tensor_scalar_mul(r1[:, :], psA[0:1, :], inv),
                    waits=[("st", stats_done)], inc="sv")
        op("vector", lambda e: e.tensor_mul(r2[:, :], r1[:, :], r1[:, :]))
        v_var = op("vector",
                   lambda e: e.scalar_tensor_tensor(
                       r2[:, :], psB[0:1, :], inv, r2[:, :],
                       ALU.mult, ALU.subtract),
                   inc="sv")
        s_sq = op("scalar",
                  lambda e: e.activation(r3[:, :], r2[:, :], AF.Sqrt,
                                         bias=epsc[0:1, :]),
                  waits=[("sv", v_var)], inc="ss")
        v_rec = op("vector", lambda e: e.reciprocal(r2[:, :], r3[:, :]),
                   waits=[("ss", s_sq)], inc="sv")

        rt_mm = op("tensor",
                   lambda e: e.matmul(psA[:, 1984:2016], ltri_sb[:, :],
                                      lastc[:, :], start=True, stop=True),
                   waits=[("st", stats_done), ("sv", v_lastc),
                          ("sv", v_mean)], inc="st")
        s_ctot = op("scalar",
                    lambda e: e.copy(ctot32[:, :], psA[:, 1984:2016]),
                    waits=[("st", rt_mm)], inc="ss")

        bc_done = None
        for tb in range(NTB):
            op("tensor",
               lambda e, tb=tb: e.matmul(
                   psA[:, tb * TB:(tb + 1) * TB], ones1f[:, :],
                   r1[0:1, tb * TB:(tb + 1) * TB], start=True, stop=True),
               waits=[("sv", v_rec), ("ss", s_ctot)])
        for tb in range(NTB):
            bc_done = op(
                "tensor",
                lambda e, tb=tb: e.matmul(
                    psB[:, tb * TB:(tb + 1) * TB], ones1f[:, :],
                    r2[0:1, tb * TB:(tb + 1) * TB], start=True, stop=True),
                inc="st" if tb == NTB - 1 else None)
        s_mean = op("scalar", lambda e: e.copy(meanbc[:, :], psA[:, :]),
                    waits=[("st", bc_done)], inc="ss")
        v_rstd = op("vector", lambda e: e.tensor_copy(rstdbc[:, :], psB[:, :]),
                    waits=[("st", bc_done)], inc="sv")

        vadd = {}
        for e_i in range(NE):
            sl64 = slice(e_i * 64, (e_i + 1) * 64)
            vadd[e_i] = op(
                "vector",
                lambda e, e_i=e_i, sl64=sl64: e.tensor_scalar(
                    mincl_all[:, sl64], mincl_all[:, sl64],
                    ctot32[:, e_i:e_i + 1],
                    float(e_i * CAP), ALU.add, ALU.add),
                waits=[("ss", s_ctot), ("sv", scan_done)], inc="sv")
        g_fin = None
        for e_i in range(NE):
            sl64 = slice(e_i * 64, (e_i + 1) * 64)
            op("gpsimd",
               lambda e, e_i=e_i: e.tensor_single_scalar(
                   m2[:, :], idxf[:, :], float(e_i), ALU.is_equal),
               waits=[("sv", vadd[e_i])])
            op("gpsimd",
               lambda e, sl64=sl64: e.tensor_mul(mtmp[:, :], m2[:, :],
                                                 mincl_all[:, sl64]))
            op("gpsimd",
               lambda e: e.tensor_sub(mtmp[:, :], mtmp[:, :], m2[:, :]))
            g_fin = op("gpsimd",
                       lambda e: e.tensor_add(macc[:, :], macc[:, :],
                                              mtmp[:, :]),
                       inc="sg")
        v_addr = op("vector", lambda e: e.tensor_copy(addr_i[:, :],
                                                      macc[:, :]),
                    waits=[("sg", g_fin)], inc="sv")
        vz = op("vector", lambda e: e.memset(val[:, :], 0.0), inc="sv")
        vlast = vz
        for c, srcv in ((0, tok_sb), (1, srw_sb), (2, tkw_sb)):
            vlast = op(
                "vector",
                lambda e, c=c, srcv=srcv: e.tensor_copy(
                    val[:, :].rearrange(
                        "p (j c) -> p j c", c=4)[:, :, c:c + 1],
                    srcv[:, :].rearrange("p (j o) -> p j o", o=1)),
                waits=[("sv", vz), ("sx", sx_small)], inc="sv")
        glist_ready = 0
        for j in range(64):
            glist_ready = op(
                "gpsimd",
                lambda e, j=j: e.indirect_dma_start(
                    glist,
                    bass.IndirectOffsetOnAxis(ap=addr_i[:, j:j + 1], axis=0),
                    val[:, 4 * j:4 * j + 4], None),
                waits=[("sv", v_addr), ("sv", vlast), ("sw", gl_init)],
                inc="sw", amt=16)

        w0 = [("ss", s_mean), ("sv", v_rstd)]
        xdma2 = {0: xdma_op(0, 0, w0), 1: xdma_op(1, T, w0)}
        xn_dc = {}
        conv_ev = {}
        cv_st = {}
        ci = 0
        for dc in range(NDC):
            sl = (dc % 2) * T
            ysl = (dc % 2) * T
            w = [("sx", xdma2[dc]), ("ss", s_mean), ("sv", v_rstd)]
            op("vector",
               lambda e, sl=sl: e.tensor_sub(
                   tmp1[:, :], x_stage[:, sl:sl + T], meanbc[:, :]),
               waits=w)
            xn_dc[dc] = op(
                "vector",
                lambda e, dc=dc: e.tensor_mul(
                    xn[:, dc * T:(dc + 1) * T], tmp1[:, :], rstdbc[:, :]),
                inc="sv")
            for tb in range(NTB):
                q = ci % 4
                wmm = [("sv", xn_dc[dc]), ("ss", s_mean), ("ss", s_ctot),
                       ("sw", cwd_l)]
                if ci >= 4:
                    wmm.append(("ss", conv_ev[ci - 4]))
                base = dc * T + tb * TB
                op("tensor",
                   lambda e, dc=dc, q=q, base=base: e.matmul(
                       psA[:, q * TB:(q + 1) * TB],
                       cwd_sb[:, dc * 384 + 128:dc * 384 + 256],
                       xn[:, base:base + TB],
                       start=True, stop=False),
                   waits=wmm)
                if tb == 0:
                    op("tensor",
                       lambda e, dc=dc, q=q, base=base: e.matmul(
                           psA[:, q * TB + 1:(q + 1) * TB],
                           cwd_sb[:, dc * 384:dc * 384 + 128],
                           xn[:, base:base + TB - 1],
                           start=False, stop=False))
                else:
                    op("tensor",
                       lambda e, dc=dc, q=q, base=base: e.matmul(
                           psA[:, q * TB:(q + 1) * TB],
                           cwd_sb[:, dc * 384:dc * 384 + 128],
                           xn[:, base - 1:base + TB - 1],
                           start=False, stop=False))
                if tb == NTB - 1:
                    cmm = op(
                        "tensor",
                        lambda e, dc=dc, q=q, base=base: e.matmul(
                            psA[:, q * TB:(q + 1) * TB - 1],
                            cwd_sb[:, dc * 384 + 256:dc * 384 + 384],
                            xn[:, base + 1:base + TB],
                            start=False, stop=True),
                        inc="st")
                else:
                    cmm = op(
                        "tensor",
                        lambda e, dc=dc, q=q, base=base: e.matmul(
                            psA[:, q * TB:(q + 1) * TB],
                            cwd_sb[:, dc * 384 + 256:dc * 384 + 384],
                            xn[:, base + 1:base + TB + 1],
                            start=False, stop=True),
                        inc="st")
                wev = [("st", cmm)]
                if dc >= 2 and tb == 0:
                    wev.append(("sx", cv_st[dc - 2]))
                conv_ev[ci] = op(
                    "scalar",
                    lambda e, dc=dc, q=q, ysl=ysl, tb=tb: e.activation(
                        ybuf2[:, ysl + tb * TB:ysl + (tb + 1) * TB],
                        psA[:, q * TB:(q + 1) * TB], AF.Identity,
                        bias=cbp_sb[:, dc:dc + 1]),
                    waits=wev, inc="ss")
                ci += 1
            cv_st[dc] = op(
                "sync",
                lambda e, dc=dc, ysl=ysl: e.dma_start(
                    conv_h[:, dc * T:(dc + 1) * T], ybuf2[:, ysl:ysl + T]),
                waits=[("ss", conv_ev[ci - 1])], inc="sx", amt=16)
            if dc + 2 < NDC:
                xdma2[dc + 2] = xdma_op(dc + 2, (dc % 2) * T,
                                        [("sv", xn_dc[dc])])
        conv_ev_last = conv_ev[ci - 1]

        psS.close()
        p2ps = contextlib.ExitStack()
        mmA = p2ps.enter_context(nc.psum_tensor("mmA", [128, 2048], F32))
        tin = p2ps.enter_context(nc.psum_tensor("tin", [128, 2048], BF16))
        otB = p2ps.enter_context(nc.psum_tensor("otB", [128, 2048], BF16))

        ps_guard = [("ss", s_mean), ("sv", v_rstd), ("ss", s_ctot)]
        tin_ev = {}
        xtm_st = {}
        gi = 0
        for tt in range(NDC):
            for g in range(4):
                ch = gi % 2
                wv = list(ps_guard) + [("sw", idt_l)]
                for q in range(4):
                    dc = g * 4 + q
                    wv.append(("sv", xn_dc[dc]))
                if gi >= 2:
                    wv.append(("ss", tin_ev[gi - 2]))
                tl = None
                for q in range(4):
                    dc = g * 4 + q
                    tl = op("tensor",
                            lambda e, dc=dc, tt=tt, ch=ch, q=q: e.transpose(
                                tin[:, ch * 1024 + q * 128:
                                    ch * 1024 + (q + 1) * 128],
                                xn[:, dc * T + tt * 128:dc * T + (tt + 1) * 128],
                                idt_sb[:, :]),
                            waits=wv, inc="st" if q == 3 else None)
                wev = [("st", tl)]
                if tt >= 2 and g == 0:
                    wev.append(("sx", xtm_st[tt - 2]))
                tin_ev[gi] = op(
                    "scalar",
                    lambda e, tt=tt, g=g, ch=ch: e.copy(
                        tstage[:, (tt % 2) * 2048 + g * 512:
                               (tt % 2) * 2048 + (g + 1) * 512],
                        tin[:, ch * 1024:ch * 1024 + 512]),
                    waits=wev, inc="ss")
                gi += 1
            xtm_st[tt] = op(
                "sync",
                lambda e, tt=tt: e.dma_start(
                    xn_tm[tt * 128:(tt + 1) * 128, :],
                    tstage[:, (tt % 2) * 2048:(tt % 2 + 1) * 2048]),
                waits=[("ss", tin_ev[gi - 1])], inc="sx", amt=16)
        xn_tm_ready = xtm_st[NDC - 1]
        xtm_tin_last = tin_ev[gi - 1]

        flush_block("p1")
        ln_scope.close()

        barrier_all()
        with (
            nc.sbuf_tensor("glt", [128, 2 * 4 * NCT], F32) as glt,
            nc.sbuf_tensor("gidx", [128, 2 * NCT], I32) as gidx,
            nc.sbuf_tensor("srow", [128, 4 * NCT], I32) as srow,
            nc.sbuf_tensor("xgst", [128, 2 * NCT * 2048], BF16) as xgst,
            nc.sbuf_tensor("xg_fm", [128, 2 * 16 * CAP], BF16) as xg_fm,
            nc.sbuf_tensor("hmid", [128, NIC * CAP], BF16) as hmid,
            nc.sbuf_tensor("silu_t", [128, 2 * CAP], BF16) as silu_t,
            nc.sbuf_tensor("o_fm", [128, 16 * CAP], BF16) as o_fm,
            nc.sbuf_tensor("o_sc", [128, 4 * 2048], BF16) as o_sc,
        ):
            z0 = op("gpsimd", lambda e: e.memset(xgst[:, :], 0.0),
                    inc="sg")

            glt_l = {}
            drv_done = {}
            gg_done = {}
            tinm_last = {}
            tinm_ev = {}
            gu_last = {}
            hmid_done = {}
            hmid_used = {}
            dn_ev = {}
            tout_last = {}
            osc_ev = {}
            sc_done = {}
            glt_used = {}

            def emit_gather(e_i):
                par = e_i % 2
                sp4 = e_i % 4
                w = [("sw", glist_ready)]
                if e_i - 2 in glt_used:
                    w.append(("sv", glt_used[e_i - 2]))
                glt_l[e_i] = op(
                    "sync",
                    lambda e, e_i=e_i, par=par: e.dma_start(
                        glt[:, par * 4 * NCT:(par + 1) * 4 * NCT].rearrange(
                            "p (ct v) -> p ct v", v=4),
                        glist[e_i * CAP:(e_i + 1) * CAP, :].rearrange(
                            "(ct p) v -> p ct v", p=128)),
                    waits=w, inc="sx", amt=16)
                g3 = glt[:, par * 4 * NCT:(par + 1) * 4 * NCT].rearrange(
                    "p (ct v) -> p ct v", v=4)
                wv = [("sx", glt_l[e_i])]
                if (e_i - 2, NCT - 1) in gg_done:
                    wv.append(("sw", gg_done[(e_i - 2, NCT - 1)]))
                if (e_i - 4, NCT - 1) in sc_done:
                    wv.append(("sw", sc_done[(e_i - 4, NCT - 1)]))
                op("vector",
                   lambda e, g3=g3, par=par: e.tensor_copy(
                       gidx[:, par * NCT:(par + 1) * NCT].rearrange(
                           "p (ct o) -> p ct o", o=1),
                       g3[:, :, 0:1]),
                   waits=wv)
                drv_done[e_i] = op(
                    "vector",
                    lambda e, g3=g3, sp4=sp4: e.tensor_copy(
                        srow[:, sp4 * NCT:(sp4 + 1) * NCT].rearrange(
                            "p (ct o) -> p ct o", o=1),
                        g3[:, :, 1:2]),
                    inc="sv")
                for ct in range(NCT):
                    w = [("sv", drv_done[e_i]), ("sg", z0),
                         ("sx", xn_tm_ready)]
                    if e_i - 2 in tinm_last:
                        w.append(("st", tinm_last[e_i - 2]))
                    gg_done[(e_i, ct)] = op(
                        "gpsimd",
                        lambda e, par=par, ct=ct:
                        e.indirect_dma_start(
                            xgst[:, (par * NCT + ct) * 2048:
                                 (par * NCT + ct + 1) * 2048], None,
                            xn_tm,
                            bass.IndirectOffsetOnAxis(
                                ap=gidx[:, par * NCT + ct:
                                        par * NCT + ct + 1], axis=0)),
                        waits=w, inc="sw", amt=16)

            def emit_tin(e_i):
                par = e_i % 2
                for hc in range(NDC):
                    ch = hc % 2
                    wv = [("sw", gg_done[(e_i, NCT - 1)]), ("sw", idt_l)]
                    if hc >= 2:
                        wv.append(("ss", tinm_ev[(e_i, hc - 2)]))
                    elif e_i >= 1:
                        wv.append(("ss", tinm_ev[(e_i - 1, NDC - 2 + hc)]))
                    else:
                        wv.append(("ss", xtm_tin_last))
                    tl = None
                    for ct in range(NCT):
                        tl = op(
                            "tensor",
                            lambda e, par=par, ct=ct, hc=hc, ch=ch:
                            e.transpose(
                                tin[:, ch * 1024 + ct * 128:
                                    ch * 1024 + (ct + 1) * 128],
                                xgst[:, (par * NCT + ct) * 2048 + hc * 128:
                                     (par * NCT + ct) * 2048 + (hc + 1) * 128],
                                idt_sb[:, :]),
                            waits=wv, inc="st" if ct == NCT - 1 else None)
                    if hc == NDC - 1:
                        tinm_last[e_i] = tl
                    wev = [("st", tl)]
                    if e_i >= 2:
                        wev.append(("st", gu_last[e_i - 2]))
                    tinm_ev[(e_i, hc)] = op(
                        "scalar",
                        lambda e, par=par, hc=hc, ch=ch: e.copy(
                            xg_fm[:, par * 16 * CAP + hc * CAP:
                                  par * 16 * CAP + (hc + 1) * CAP],
                            tin[:, ch * 1024:ch * 1024 + NCT * 128]),
                        waits=wev, inc="ss")

            emit_gather(0)
            emit_gather(1)

            bank = [(mmA, 0), (mmA, 1), (mmA, 2), (mmA, 3)]

            def bank_ap(i):
                t, q = bank[i]
                return t[:, q * TB:q * TB + CAP]

            bank_free = [("sv", v_rstd)] * 4
            bslot = [0]

            def next_bank(extra_waits=()):
                i = bslot[0] % 4
                bslot[0] += 1
                return i, [bank_free[i]] + list(extra_waits)

            def emit_scatters(e_i):
                par = e_i % 2
                sp4 = e_i % 4
                for ct in range(NCT):
                    sc_done[(e_i, ct)] = op(
                        "gpsimd",
                        lambda e, sp4=sp4, ct=ct:
                        e.indirect_dma_start(
                            y4,
                            bass.IndirectOffsetOnAxis(
                                ap=srow[:, sp4 * NCT + ct:
                                        sp4 * NCT + ct + 1], axis=0),
                            o_sc[:, ct * 2048:(ct + 1) * 2048],
                            None),
                        waits=[("sv", osc_ev[(e_i, ct, 0)]),
                               ("sv", osc_ev[(e_i, ct, 1)])],
                        inc="sw", amt=16)

            w_gu = {0: pre_w}

            for e_i in range(NE):
                par = e_i % 2
                if e_i >= 1:
                    emit_scatters(e_i - 1)
                emit_tin(e_i)
                g_of = par * 16 * CAP
                tin_all = ("ss", tinm_ev[(e_i, NDC - 1)])
                silu_done = {}
                for ic in range(NIC):
                    if ic < 3:
                        wg, jg = w_gu[e_i][2 * ic]
                        wu_, ju = w_gu[e_i][2 * ic + 1]
                    else:
                        wg, jg = wflat_dma(e_i, ic * 2 * GU, 2048)
                        wu_, ju = wflat_dma(e_i, ic * 2 * GU + GU, 2048)
                    bi, bw = next_bank([("sw", wg), tin_all])
                    g_last = None
                    for hc in range(NDC):
                        g_last = op(
                            "tensor",
                            lambda e, jg=jg, hc=hc, bi=bi, g_of=g_of:
                            e.matmul(
                                bank_ap(bi),
                                wring[:, jg * 2048 + hc * 128:
                                      jg * 2048 + (hc + 1) * 128],
                                xg_fm[:, g_of + hc * CAP:
                                      g_of + (hc + 1) * CAP],
                                start=(hc == 0), stop=(hc == NDC - 1)),
                            waits=bw,
                            inc="st" if hc == NDC - 1 else None)
                    slot_free[jg] = ("st", g_last)
                    sp = (ic % 2) * CAP
                    hprev = hmid_done.get(
                        (e_i, ic - 2),
                        hmid_done.get((e_i - 1, NIC - 2 + ic), 0))
                    silu_done[ic] = op(
                        "scalar",
                        lambda e, bi=bi, sp=sp: e.activation(
                            silu_t[:, sp:sp + CAP], bank_ap(bi), AF.Silu),
                        waits=[("st", g_last), ("sv", hprev)],
                        inc="ss")
                    bank_free[bi] = ("ss", silu_done[ic])
                    bi2, bw2 = next_bank([("sw", wu_)])
                    u_last = None
                    for hc in range(NDC):
                        u_last = op(
                            "tensor",
                            lambda e, ju=ju, hc=hc, bi2=bi2, g_of=g_of:
                            e.matmul(
                                bank_ap(bi2),
                                wring[:, ju * 2048 + hc * 128:
                                      ju * 2048 + (hc + 1) * 128],
                                xg_fm[:, g_of + hc * CAP:
                                      g_of + (hc + 1) * CAP],
                                start=(hc == 0), stop=(hc == NDC - 1)),
                            waits=bw2,
                            inc="st" if hc == NDC - 1 else None)
                    slot_free[ju] = ("st", u_last)
                    if ic == NIC - 1:
                        gu_last[e_i] = u_last
                    hmid_done[(e_i, ic)] = op(
                        "vector",
                        lambda e, bi2=bi2, ic=ic, sp=sp: e.tensor_mul(
                            hmid[:, ic * CAP:(ic + 1) * CAP],
                            bank_ap(bi2), silu_t[:, sp:sp + CAP]),
                        waits=[("st", u_last), ("ss", silu_done[ic]),
                               ("st", hmid_used.get(e_i - 1, 0))],
                        inc="sv")
                    bank_free[bi2] = ("sv", hmid_done[(e_i, ic)])
                for hc in range(NDC):
                    wd, jd = wflat_dma(e_i, 16 * GU + hc * DD, 1024)
                    bi, bw = next_bank([("sw", wd)])
                    o_last = None
                    for ic in range(NIC):
                        o_last = op(
                            "tensor",
                            lambda e, jd=jd, ic=ic, bi=bi: e.matmul(
                                bank_ap(bi),
                                wring[:, jd * 2048 + ic * 128:
                                      jd * 2048 + (ic + 1) * 128],
                                hmid[:, ic * CAP:(ic + 1) * CAP],
                                start=(ic == 0), stop=(ic == NIC - 1)),
                            waits=bw + [("sv", hmid_done[(e_i, ic)])],
                            inc="st" if ic == NIC - 1 else None)
                    slot_free[jd] = ("st", o_last)
                    if hc == NDC - 1:
                        hmid_used[e_i] = o_last
                    wev = [("st", o_last)]
                    if e_i >= 1:
                        wev.append(("st", tout_last[e_i - 1]))
                    dn_ev[(e_i, hc)] = op(
                        "scalar",
                        lambda e, bi=bi, hc=hc: e.copy(
                            o_fm[:, hc * CAP:(hc + 1) * CAP], bank_ap(bi)),
                        waits=wev, inc="ss")
                    bank_free[bi] = ("ss", dn_ev[(e_i, hc)])
                dn_all = ("ss", dn_ev[(e_i, NDC - 1)])
                for ct in range(NCT):
                    for half in range(2):
                        wv = [dn_all, ("sw", idt_l)]
                        prev = (e_i, ct - 1, half)
                        if ct == 0:
                            prev = (e_i - 1, NCT - 1, half)
                        if prev in osc_ev:
                            wv.append(("sv", osc_ev[prev]))
                        if e_i == 0:
                            wv.append(("sv", v_rstd))
                        tl = None
                        for q in range(8):
                            hc = half * 8 + q
                            tl = op(
                                "tensor",
                                lambda e, hc=hc, ct=ct, half=half, q=q:
                                e.transpose(
                                    otB[:, half * 1024 + q * 128:
                                        half * 1024 + (q + 1) * 128],
                                    o_fm[:, hc * CAP + ct * 128:
                                         hc * CAP + (ct + 1) * 128],
                                    idt_sb[:, :]),
                                waits=wv, inc="st" if q == 7 else None)
                        if ct == NCT - 1 and half == 1:
                            tout_last[e_i] = tl
                        wc = par * 4 * NCT + ct * 4 + 2
                        wev = [("st", tl)]
                        if (e_i - 1, ct) in sc_done:
                            wev.append(("sw", sc_done[(e_i - 1, ct)]))
                        osc_ev[(e_i, ct, half)] = op(
                            "vector",
                            lambda e, ct=ct, half=half, wc=wc:
                            e.tensor_scalar_mul(
                                o_sc[:, ct * 2048 + half * 1024:
                                     ct * 2048 + (half + 1) * 1024],
                                otB[:, half * 1024:(half + 1) * 1024],
                                glt[:, wc:wc + 1]),
                            waits=wev, inc="sv")
                glt_used[e_i] = osc_ev[(e_i, NCT - 1, 1)]
                if e_i + 1 < NE:
                    w_gu[e_i + 1] = [
                        wflat_dma(e_i + 1, 0, 2048),
                        wflat_dma(e_i + 1, GU, 2048),
                        wflat_dma(e_i + 1, 2 * GU, 2048),
                        wflat_dma(e_i + 1, 3 * GU, 2048),
                        wflat_dma(e_i + 1, 4 * GU, 2048),
                        wflat_dma(e_i + 1, 5 * GU, 2048)]
                if e_i + 2 < NE:
                    emit_gather(e_i + 2)
            emit_scatters(NE - 1)
            scatters_done = cnt["sw"]
            flush_block("moe")
        p2ps.close()

        barrier_all()
        with (
            nc.psum_tensor("psA2", [128, 2048], F32) as psA2,
            nc.psum_tensor("psB2", [128, 2048], F32) as psB2,
        ):
            at_scope = contextlib.ExitStack()
            wg1_bf = at_scope.enter_context(
                nc.sbuf_tensor("wg1_bf", [128, 8192], BF16))
            h_all = at_scope.enter_context(
                nc.sbuf_tensor("h_all", [128, NDHC * T], BF16))
            gates = at_scope.enter_context(
                nc.sbuf_tensor("gates", [128, T], F32))
            ug = at_scope.enter_context(nc.sbuf_tensor("ug", [128, T], F32))
            vg = at_scope.enter_context(nc.sbuf_tensor("vg", [128, T], F32))
            gost = at_scope.enter_context(
                nc.sbuf_tensor("gost", [128, 2 * T], BF16))
            d_wg1 = op("gpsimd", lambda e: e.dma_start(wg1_bf[:, :], wg1d),
                       inc="sw", amt=16)
            h_done = {}
            relu_done = {}
            ps_of = {0: psA2, 1: psB2}
            for dhc in range(NDHC):
                ps = ps_of[dhc % 2]
                for dc in range(NDC):
                    for tb in range(NTB):
                        h_done[dhc] = op(
                            "tensor",
                            lambda e, ps=ps, dc=dc, dhc=dhc, tb=tb:
                            e.matmul(
                                ps[:, tb * TB:(tb + 1) * TB],
                                wg1_bf[:, dc * 512 + dhc * 128:
                                       dc * 512 + (dhc + 1) * 128],
                                xn[:, dc * T + tb * TB:
                                   dc * T + (tb + 1) * TB],
                                start=(dc == 0), stop=(dc == NDC - 1)),
                            waits=[("sw", d_wg1),
                                   ("ss", relu_done.get(dhc - 2, 0))],
                            inc="st" if (dc == NDC - 1 and tb == NTB - 1)
                            else None)
                relu_done[dhc] = op(
                    "scalar",
                    lambda e, ps=ps, dhc=dhc: e.activation(
                        h_all[:, dhc * T:(dhc + 1) * T], ps[:, :], AF.Relu,
                        bias=bg1_sb[:, dhc:dhc + 1]),
                    waits=[("st", h_done[dhc])], inc="ss")
            h_ready = relu_done[NDHC - 1]

            psA_free = ("ss", relu_done.get(NDHC - 2, 0))
            psB_free = ("ss", h_ready)
            go_done = {}
            go_st = {}
            for rc in range(NRC):
                v2, jg2 = wblock_dma(wg2d[rc])
                vu, ju_ = wblock_dma(wud[rc])
                vv, jv_ = wblock_dma(wvd[rc])

                gmm = None
                for dhc in range(NDHC):
                    for tb in range(NTB):
                        gmm = op(
                            "tensor",
                            lambda e, jg2=jg2, dhc=dhc, tb=tb: e.matmul(
                                psB2[:, tb * TB:(tb + 1) * TB],
                                wring[:, jg2 * 2048 + dhc * 128:
                                      jg2 * 2048 + (dhc + 1) * 128],
                                h_all[:, dhc * T + tb * TB:
                                      dhc * T + (tb + 1) * TB],
                                start=(dhc == 0), stop=(dhc == NDHC - 1)),
                            waits=[("sw", v2), ("ss", h_ready), psB_free],
                            inc="st" if (dhc == NDHC - 1 and tb == NTB - 1)
                            else None)
                slot_free[jg2] = ("st", gmm)
                s_gate = op(
                    "scalar",
                    lambda e, rc=rc: e.activation(
                        gates[:, :], psB2[:, :], AF.Sigmoid,
                        bias=bg2_sb[:, rc:rc + 1]),
                    waits=[("st", gmm)], inc="ss")
                psB_free = ("ss", s_gate)
                umm = None
                for dc in range(NDC):
                    for tb in range(NTB):
                        umm = op(
                            "tensor",
                            lambda e, ju_=ju_, dc=dc, tb=tb: e.matmul(
                                psA2[:, tb * TB:(tb + 1) * TB],
                                wring[:, ju_ * 2048 + dc * 128:
                                      ju_ * 2048 + (dc + 1) * 128],
                                xn[:, dc * T + tb * TB:
                                   dc * T + (tb + 1) * TB],
                                start=(dc == 0), stop=(dc == NDC - 1)),
                            waits=[("sw", vu), psA_free],
                            inc="st" if (dc == NDC - 1 and tb == NTB - 1)
                            else None)
                slot_free[ju_] = ("st", umm)
                v_ug = op("vector",
                          lambda e, rc=rc: e.scalar_tensor_tensor(
                              ug[:, :], psA2[:, :], bup_sb[:, rc:rc + 1],
                              gates[:, :], ALU.add, ALU.mult),
                          waits=[("st", umm), ("ss", s_gate)], inc="sv")
                psA_free = ("sv", v_ug)
                vmm = None
                for dc in range(NDC):
                    for tb in range(NTB):
                        vmm = op(
                            "tensor",
                            lambda e, jv_=jv_, dc=dc, tb=tb: e.matmul(
                                psB2[:, tb * TB:(tb + 1) * TB],
                                wring[:, jv_ * 2048 + dc * 128:
                                      jv_ * 2048 + (dc + 1) * 128],
                                xn[:, dc * T + tb * TB:
                                   dc * T + (tb + 1) * TB],
                                start=(dc == 0), stop=(dc == NDC - 1)),
                            waits=[("sw", vv), psB_free],
                            inc="st" if (dc == NDC - 1 and tb == NTB - 1)
                            else None)
                slot_free[jv_] = ("st", vmm)
                v_vg = op("vector",
                          lambda e, rc=rc: e.scalar_tensor_tensor(
                              vg[:, :], psB2[:, :], bvp_sb[:, rc:rc + 1],
                              gates[:, :], ALU.add, ALU.mult),
                          waits=[("st", vmm)], inc="sv")
                psB_free = ("sv", v_vg)
                op("vector",
                   lambda e: e.tensor_tensor_scan(
                       ug[:, :], ug[:, :],
                       zcol[:, :].broadcast_to((128, T)), 0.0,
                       ALU.add, ALU.add))
                wgo = []
                if rc >= 2:
                    wgo.append(("sx", go_st[rc - 2]))
                go_done[rc] = op(
                    "vector",
                    lambda e, rc=rc: e.tensor_mul(
                        gost[:, (rc % 2) * T:(rc % 2 + 1) * T],
                        ug[:, :], vg[:, :]),
                    waits=wgo, inc="sv")
                for tb in range(NTB):
                    go_st[rc] = op(
                        "sync",
                        lambda e, rc=rc, tb=tb: e.dma_start(
                            go_h[tb, :, rc, :],
                            gost[:, (rc % 2) * T + tb * TB:
                                 (rc % 2) * T + (tb + 1) * TB]),
                        waits=[("sv", go_done[rc])], inc="sx", amt=16)
            go_ready = go_st[NRC - 1]
            flush_block("attn")
            at_scope.close()
        xnsc.close()

        barrier_all()
        with (
            nc.psum_tensor("psU", [128, 3072], F32) as psU,
            nc.psum_tensor("ptY", [128, 2048], BF16) as ptY,
            nc.sbuf_tensor("wud_all", [128, NDC * 2048], BF16) as wud_all,
            nc.sbuf_tensor("go_stage", [128, 2 * NRC * TB], BF16) as go_stage,
            nc.sbuf_tensor("ysum", [128, 2 * 4 * 2048], BF16) as ysum,
            nc.sbuf_tensor("ystage", [128, 4 * 2048], BF16) as ystage,
            nc.sbuf_tensor("x_st2", [128, 6 * TB], F32) as x_st2,
            nc.sbuf_tensor("cstr", [128, 6 * TB], BF16) as cstr,
            nc.sbuf_tensor("ostg", [128, 6 * TB], F32) as ostg,
        ):
            wud_l = {}
            for dc in range(NDC):
                wud_l[dc] = op(
                    "gpsimd",
                    lambda e, dc=dc: e.dma_start(
                        wud_all[:, dc * 2048:(dc + 1) * 2048], wud[dc]),
                    inc="sw", amt=16)

            yld = {}
            ysum_done = {}
            gol = {}
            f1d = {}
            f2d = {}
            out_st = {}

            def emit_prepass(tb):
                par = tb % 2
                gw = [("sx", go_ready)]
                if tb >= 2:
                    gw.append(("st", uv_l[(tb - 2, NDC - 1)]))
                gol[tb] = op(
                    "scalar",
                    lambda e, tb=tb, par=par: e.dma_start(
                        go_stage[:, par * NRC * TB:(par + 1) * NRC * TB],
                        go_h[tb].rearrange("p rc c -> p (rc c)")),
                    waits=gw, inc="sy", amt=16)
                for k in range(NK):
                    i = tb * NK + k
                    w = [("sw", scatters_done)]
                    if i >= 1:
                        w.append(("sv", ysum_done[((i - 1) // NK,
                                                   (i - 1) % NK)]))
                    yld[(tb, k)] = op(
                        "scalar",
                        lambda e, tb=tb, k=k: e.dma_start(
                            ystage[:, :].rearrange("p (s f) -> p s f", s=4),
                            y4[k * T + tb * 512:k * T + (tb + 1) * 512, :]
                            .rearrange("(s p) f -> p s f", p=128)),
                        waits=w, inc="sy", amt=16)
                    par2 = (tb % 2) * 8192
                    wv = [("sy", yld[(tb, k)])]
                    if k == 0 and tb >= 2:
                        wv.append(("sv", f2d[(tb - 2, NDC - 1)]))
                    if k == 0:
                        ysum_done[(tb, k)] = op(
                            "vector",
                            lambda e, par2=par2: e.tensor_copy(
                                ysum[:, par2:par2 + 8192], ystage[:, :]),
                            waits=wv, inc="sv")
                    else:
                        ysum_done[(tb, k)] = op(
                            "vector",
                            lambda e, par2=par2: e.tensor_add(
                                ysum[:, par2:par2 + 8192],
                                ysum[:, par2:par2 + 8192], ystage[:, :]),
                            waits=wv, inc="sv")

            uv_l = {}
            emit_prepass(0)
            emit_prepass(1)
            for tb in range(NTB):
                par = tb % 2
                g_of = par * NRC * TB
                y_of = par * 8192
                for dc in range(NDC):
                    i = tb * NDC + dc
                    sl = (i % 6) * TB
                    wld = []
                    if i >= 6:
                        wld.append(("sg", f1d[((i - 6) // NDC,
                                               (i - 6) % NDC)]))
                    op("sync",
                       lambda e, dc=dc, tb=tb, sl=sl: e.dma_start(
                           x_st2[:, sl:sl + TB],
                           xt[dc * 128:(dc + 1) * 128,
                              tb * TB:(tb + 1) * TB]),
                       waits=wld, inc="sx", amt=16)
                    cl = op("sync",
                            lambda e, dc=dc, tb=tb, sl=sl: e.dma_start(
                                cstr[:, sl:sl + TB],
                                conv_h[:, dc * T + tb * TB:
                                       dc * T + (tb + 1) * TB]),
                            inc="sx", amt=16)
                    uvw = [("sw", wud_l[dc]), ("sy", gol[tb])]
                    if i >= 6:
                        uvw.append(("sg", f1d[((i - 6) // NDC,
                                               (i - 6) % NDC)]))
                    uv = None
                    for rc in range(NRC):
                        uv = op(
                            "tensor",
                            lambda e, dc=dc, rc=rc, g_of=g_of, sl=sl:
                            e.matmul(
                                psU[:, sl:sl + TB],
                                wud_all[:, dc * 2048 + rc * 128:
                                        dc * 2048 + (rc + 1) * 128],
                                go_stage[:, g_of + rc * TB:
                                         g_of + (rc + 1) * TB],
                                start=(rc == 0), stop=(rc == NRC - 1)),
                            waits=uvw,
                            inc="st" if rc == NRC - 1 else None)
                    uv_l[(tb, dc)] = uv
                    tw = [("sv", ysum_done[(tb, NK - 1)]), ("sw", idt_l)]
                    if i >= 2:
                        tw.append(("sv", f2d[((i - 2) // NDC,
                                              (i - 2) % NDC)]))
                    sly = (i % 2) * 1024
                    ty = None
                    for s in range(4):
                        ty = op(
                            "tensor",
                            lambda e, s=s, dc=dc, y_of=y_of, sly=sly:
                            e.transpose(
                                ptY[:, sly + s * 128:sly + (s + 1) * 128],
                                ysum[:, y_of + s * 2048 + dc * 128:
                                     y_of + s * 2048 + (dc + 1) * 128],
                                idt_sb[:, :]),
                            waits=tw, inc="st" if s == 3 else None)
                    wf = [("st", uv), ("sx", cl)]
                    if i >= 6:
                        wf.append(("so", out_st[i - 6]))
                    fa = op("vector",
                            lambda e, sl=sl: e.tensor_add(
                                ostg[:, sl:sl + TB], psU[:, sl:sl + TB],
                                x_st2[:, sl:sl + TB]),
                            waits=wf, inc="sv")
                    f1d[(tb, dc)] = op(
                        "gpsimd",
                        lambda e, sl=sl: e.tensor_add(
                            ostg[:, sl:sl + TB], ostg[:, sl:sl + TB],
                            cstr[:, sl:sl + TB]),
                        waits=[("sv", fa)], inc="sg")
                    f2d[(tb, dc)] = op(
                        "vector",
                        lambda e, sl=sl, sly=sly: e.tensor_add(
                            ostg[:, sl:sl + TB], ostg[:, sl:sl + TB],
                            ptY[:, sly:sly + TB]),
                        waits=[("st", ty), ("sg", f1d[(tb, dc)])],
                        inc="sv")
                    out_st[i] = op(
                        "gpsimd",
                        lambda e, dc=dc, tb=tb, sl=sl: e.dma_start(
                            outt[dc * 128:(dc + 1) * 128,
                                 tb * TB:(tb + 1) * TB],
                            ostg[:, sl:sl + TB]),
                        waits=[("sv", f2d[(tb, dc)])], inc="so", amt=16)
                if tb + 2 < NTB:
                    emit_prepass(tb + 2)
            wait_only("sync", [("so", cnt["so"])])
            flush_block("final")

    return nc


def _pack_t4(a):
    n = a.shape[0] // 128
    c = a.shape[1] // 128
    return np.ascontiguousarray(
        a.reshape(n, 128, c, 128).transpose(2, 1, 0, 3).reshape(
            c, 128, n * 128))


def kernel(L0_attn_x, L0_attn_Wu, L0_attn_bu, L0_attn_Wv, L0_attn_bv,
           L0_attn_Wg1, L0_attn_bg1, L0_attn_Wg2, L0_attn_bg2,
           L0_attn_conv_w, L0_attn_conv_b,
           L0_ffn_gate_weight, L0_ffn_up_weight, L0_ffn_down_weight,
           L0_ffn_topk_idx, L0_ffn_topk_weight):
    x = np.asarray(L0_attn_x, dtype=np.float32)
    Wu = np.asarray(L0_attn_Wu, dtype=np.float32)
    Wv = np.asarray(L0_attn_Wv, dtype=np.float32)
    Wg1 = np.asarray(L0_attn_Wg1, dtype=np.float32)
    Wg2 = np.asarray(L0_attn_Wg2, dtype=np.float32)
    bu = np.asarray(L0_attn_bu, dtype=np.float32)
    bv = np.asarray(L0_attn_bv, dtype=np.float32)
    bg1 = np.asarray(L0_attn_bg1, dtype=np.float32)
    bg2 = np.asarray(L0_attn_bg2, dtype=np.float32)
    convw = np.asarray(L0_attn_conv_w, dtype=np.float32)
    convb = np.asarray(L0_attn_conv_b, dtype=np.float32)
    gw = np.asarray(L0_ffn_gate_weight, dtype=np.float32)
    uw = np.asarray(L0_ffn_up_weight, dtype=np.float32)
    dw = np.asarray(L0_ffn_down_weight, dtype=np.float32)
    tidx = np.asarray(L0_ffn_topk_idx, dtype=np.int32)
    tw = np.asarray(L0_ffn_topk_weight, dtype=np.float32)

    B = x.shape[0]

    wud = _pack_t4(np.ascontiguousarray(Wu.T))
    wvd = _pack_t4(np.ascontiguousarray(Wv.T))
    wg2d = _pack_t4(np.ascontiguousarray(Wg2.T))
    wg1d = np.ascontiguousarray(
        Wg1.T.reshape(16, 128, 512).transpose(1, 0, 2).reshape(128, 8192))
    moe_blocks = []
    for e in range(NE):
        gt = _pack_t4(np.ascontiguousarray(gw[e].T))
        ut = _pack_t4(np.ascontiguousarray(uw[e].T))
        dt = _pack_t4(np.ascontiguousarray(dw[e].T))
        parts = []
        for ic in range(NIC):
            parts.append(gt[ic].ravel())
            parts.append(ut[ic].ravel())
        for hc in range(NDC):
            parts.append(dt[hc].ravel())
        moe_blocks.append(np.concatenate(parts))
    wmoe_np = np.ascontiguousarray(np.stack(moe_blocks))
    del moe_blocks

    bup_np = np.ascontiguousarray(bu.reshape(16, 128).T)
    bvp_np = np.ascontiguousarray(bv.reshape(16, 128).T)
    bg2_np = np.ascontiguousarray(bg2.reshape(16, 128).T)
    bg1_np = np.ascontiguousarray(bg1.reshape(4, 128).T)
    cwp_np = np.ascontiguousarray(
        convw[:, 0, :].reshape(16, 128, 3).transpose(1, 0, 2).reshape(128, 48))
    cbp_np = np.ascontiguousarray(convb.reshape(16, 128).T)

    s_all = np.arange(NSLOT)
    t_all = s_all // NK
    k_all = s_all % NK
    tok_np = np.ascontiguousarray(t_all.astype(np.float32).reshape(128, 64))
    srow_np = np.ascontiguousarray(
        (k_all * T + t_all).astype(np.float32).reshape(128, 64))
    ltri_np = np.ascontiguousarray(np.triu(np.ones((128, 128), np.float32), 1))
    idt_np = np.eye(128, dtype=np.float32)
    cwd_np = np.zeros((128, 16 * 384), np.float32)
    for dc in range(NDC):
        for k in range(3):
            cwd_np[np.arange(128), dc * 384 + k * 128 + np.arange(128)] = (
                convw[dc * 128 + np.arange(128), 0, k])

    nc = _build_program()
    in_maps = []
    for b in range(B):
        ts = tidx[b * T:(b + 1) * T].reshape(128, 64)
        ws = tw[b * T:(b + 1) * T].reshape(128, 64)
        in_maps.append({
            "xt": np.ascontiguousarray(x[b].T),
            "wud": wud, "wvd": wvd, "wg2d": wg2d, "wg1d": wg1d,
            "wmoe": wmoe_np,
            "bup": bup_np, "bvp": bvp_np, "bg2p": bg2_np, "bg1p": bg1_np,
            "cwp": cwp_np, "cbp": cbp_np,
            "tki": np.ascontiguousarray(ts),
            "tkw": np.ascontiguousarray(ws),
            "tokid": tok_np, "srowid": srow_np, "ltri": ltri_np,
            "idt": idt_np, "cwd": cwd_np,
        })
    res = run_bass_kernel_spmd(nc, in_maps, core_ids=list(range(B)))
    out = np.empty((B, T, D), dtype=np.float32)
    for b in range(B):
        out[b] = np.asarray(res.results[b]["outt"]).T
    return out
